# version 04
# baseline (speedup 1.0000x reference)
import numpy as np
import ml_dtypes
import concourse.bass as bass
import concourse.mybir as mybir

F32 = mybir.dt.float32
BF16 = mybir.dt.bfloat16
U32 = mybir.dt.uint32
AF = mybir.ActivationFunctionType
ALU = mybir.AluOpType
AX = mybir.AxisListType

D = 1024
NCT = 8
NALL = 16384
NZT = 128
CAP = 5120.0
NITER = 60
bfd = ml_dtypes.bfloat16

UNITS = [("sh", q, None) for q in range(4)] + [("ex", e, h) for e in range(8) for h in range(2)]
NU = len(UNITS)


class CE:

    def __init__(self, eng, sem):
        self.e, self.s, self.n = eng, sem, 0

    def wait_ge(self, sem, v):
        self.e.wait_ge(sem, v)

    def mark(self, sem, val=1):
        inst = self.e.nop()
        if self.n > 0:
            inst._wait_ge(self.s, self.n)
        inst.then_inc(sem, val)
        return inst

    def __getattr__(self, name):
        fn = getattr(self.e, name)

        def call(*a, **k):
            inst = fn(*a, **k)
            if self.n > 0:
                inst._wait_ge(self.s, self.n)
            self.n += 1
            inst.then_inc(self.s, 1)
            return inst

        return call


class Sem:
    def __init__(self, handle):
        self.h = handle
        self.n = 0

    def bump(self, by=1):
        self.n += by
        return self.n


class Rot:

    def __init__(self, tensors):
        self.t = tensors
        self.free = {}
        self.g = 0

    def acquire(self):
        g = self.g
        self.g += 1
        return self.t[g % len(self.t)], self.free.get(g - len(self.t)), g

    def release(self, g, sem, n):
        self.free[g] = (sem, n)


def core_T0(c):
    return 32 * ((c // 2) // 2) + 16 * ((c // 2) % 2)


def build_fused(cora=None, exchange="shared", rc=False, dbg=None):
    nc = bass.Bass("TRN2", target_bir_lowering=False, detect_race_conditions=rc)

    def din(name, shape, dt=F32):
        return nc.dram_tensor(name, shape, dt, kind="ExternalInput").ap()

    xin = din("xin", [2 * 4096, D])
    qs_cm = din("qs_cm", [128, NCT])
    ks_cm = din("ks_cm", [128, NCT])
    scal = din("scal", [1, 4])
    projT = din("projT", [128, NCT * 8])
    pb8 = din("pb8", [128, 8])
    eb8 = din("eb8", [128, 8])
    u1wb = din("u1wb", [NU, D, D], BF16)
    u2wb = din("u2wb", [NU, D, D], BF16)
    ubias = din("ubias", [128, NU * 8])
    fc2bT = din("fc2bT", [8, D])
    sh2b_row = din("sh2b_row", [1, D])
    in_wb = din("in_wb", [D, D], BF16)
    in_b_cm = din("in_b_cm", [128, 8])
    mod_w1b = din("mod_w1b", [D, 4096], BF16)
    mod_b1_cm = din("mod_b1_cm", [128, 32])
    mod_w2b = din("mod_w2b", [4096, D], BF16)
    mod_b2_cm = din("mod_b2_cm", [128, 8])
    kern_cm = din("kern_cm", [128, 8 * 64])
    out_wb = din("out_wb", [D, D], BF16)
    out_b_cm = din("out_b_cm", [128, 8])

    dbg_rank = {"p1": 1, "p3": 3, "p4": 4, None: 9}[dbg]
    okind = "ExternalOutput"
    x1d = nc.dram_tensor("x1d", [8192, D], F32,
                         kind=(okind if dbg_rank == 1 else "Internal")).ap()
    x2d = nc.dram_tensor("x2d", [8192, D], F32,
                         kind=(okind if dbg_rank == 3 else "Internal")).ap()
    x3d = nc.dram_tensor("x3d", [8192, D], F32,
                         kind=(okind if dbg_rank == 4 else "Internal")).ap()
    if dbg_rank in (1, 4):
        zdbg = nc.dram_tensor("zdbg", [128, 512], F32, kind=okind).ap()
    if dbg_rank == 9:
        y_out = nc.dram_tensor("y_out", [2048, D], F32, kind=okind).ap()
    wdram = nc.dram_tensor("wdram", [128, 256], F32).ap()
    edram = nc.dram_tensor("edram", [128, 256], F32).ap()
    if exchange == "shared":
        nonce = din("nonce", [1, 4], U32)
        shz = nc.dram_tensor("shz", [256, 1024], F32, addr_space="Shared").ap()
        shflag = nc.dram_tensor("shflag", [2, 128], U32, addr_space="Shared").ap()
    else:
        pz0 = din("pz0", [128, 512])
        pz1 = din("pz1", [128, 512])

    ident_np = np.eye(128, dtype=bfd)
    Rm_np = (np.arange(128)[:, None] - np.arange(128)[None, :]).astype(np.float32)
    Dm_np = np.broadcast_to(128.0 * np.arange(32)[None, :] - 1.0, (128, 32)).astype(np.float32).copy()
    identb = nc.inline_tensor(ident_np, "identb").ap()
    RmT = nc.inline_tensor(Rm_np, "Rm").ap()
    DmT = nc.inline_tensor(Dm_np, "Dm").ap()
    onesT = nc.inline_tensor(np.ones((1, 128), np.float32), "ones_row").ap()
    ecol_np = (np.arange(128)[:, None] // 16).astype(np.float32)
    Esel_np = (ecol_np == np.arange(8)[None, :]).astype(np.float32)
    e_colC = nc.inline_tensor(ecol_np, "e_colC").ap()
    EselC = nc.inline_tensor(Esel_np, "EselC").ap()
    EselTC = nc.inline_tensor(np.ascontiguousarray(Esel_np.T), "EselTC").ap()
    ones512C = nc.inline_tensor(np.ones((1, 512), bfd), "ones512C").ap()
    iota8C = nc.inline_tensor(np.broadcast_to(np.arange(8).astype(np.float32)[None, :],
                                              (128, 8)).copy(), "iota8C").ap()
    ident8C = nc.inline_tensor(np.eye(8, dtype=np.float32), "ident8C").ap()
    ones8x128C = nc.inline_tensor(np.ones((8, 128), np.float32), "ones8x128C").ap()
    selC = []
    for e in range(8):
        m = np.zeros((8, 128), bfd)
        m[e, :] = 1.0
        selC.append(nc.inline_tensor(m, f"sel{e}C").ap())

    from contextlib import ExitStack
    with ExitStack() as ctx:
        E = ctx.enter_context
        arV = E(nc.sbuf_tensor("arV", [128, 33280], BF16))
        arK = E(nc.sbuf_tensor("arK", [128, 32768], BF16))
        arW = E(nc.sbuf_tensor("arW", [128, 8192], BF16))
        V = arV
        K = arK
        Wp = arW[:, 0:4096]
        Qt = arW[:, 4096:5120]
        sw = arW[:, 5120:5376]
        cmA = arV[:, 0:16384]
        y_cm = arV[:, 16384:32768]
        fc1sb = arK[:, 0:16384]
        fc2sb = arK[:, 16384:32768]
        hbuf = arW[:, 0:4096]
        modu = arV
        wfix_in = arK[:, 0:8192]
        wfix_out = arK[:, 8192:16384]
        hm = arK[:, 16384:32768]
        conv_full = arK[:, 16384:32768]
        wstream = arW
        diag64 = arW[:, 0:8192]

        xbuf = E(nc.sbuf_tensor("xbuf", [128, 2 * D], F32))
        x1t = E(nc.sbuf_tensor("x1t", [128, 2 * D], F32))
        x2t = E(nc.sbuf_tensor("x2t", [128, 2 * D], F32))
        wge = xbuf
        epe = x1t
        x3t = x2t
        xproj = x2t[:].bitcast(BF16)
        lnchunk = E(nc.sbuf_tensor("lnchunk", [128, NCT * 512], BF16))
        osb = lnchunk[:, 0:4096]
        zsb = x2t[:, 0:1024]
        sall = E(nc.sbuf_tensor("sall", [128, 1024], F32))
        g8all = E(nc.sbuf_tensor("g8all", [8, 2048], BF16))
        z_sb = E(nc.sbuf_tensor("z_sb", [128, 512], F32))
        lnt = E(nc.sbuf_tensor("lnt", [128, D], BF16))
        lnzT = E(nc.sbuf_tensor("lnzT", [128, NCT * 128], BF16))
        x1t_b = x1t[:].bitcast(BF16)
        hraw = x1t_b[:, 0:1024]
        gate_sb = x1t_b[:, 1024:2048]
        wgrid = E(nc.sbuf_tensor("wgrid", [128, 256], F32))
        egrid = E(nc.sbuf_tensor("egrid", [128, 256], F32))
        identt = E(nc.sbuf_tensor("identt", [128, 128], BF16))
        Rl = x2t[:, 1024:1152]
        Dl = x2t[:, 1152:1184]
        Rm_s = x2t[:, 1440:1568]
        Dm_s = x2t[:, 1568:1600]
        ones_s = E(nc.sbuf_tensor("ones_s", [1, 128], F32))
        ones512_s = E(nc.sbuf_tensor("ones512_s", [1, 512], BF16))
        scal_s = E(nc.sbuf_tensor("scal_s", [1, 4], F32))
        scalc = E(nc.sbuf_tensor("scalc", [128, 4], F32))
        lnd = E(nc.sbuf_tensor("lnd", [128, 1], F32))
        sigd = E(nc.sbuf_tensor("sigd", [128, 1], F32))
        qs_s = E(nc.sbuf_tensor("qs_s", [128, NCT], F32))
        ks_s = E(nc.sbuf_tensor("ks_s", [128, NCT], F32))
        projT_s = E(nc.sbuf_tensor("projT_s", [128, NCT * 8], BF16))
        pb_s = E(nc.sbuf_tensor("pb_s", [128, 8], F32))
        eb_s = E(nc.sbuf_tensor("eb_s", [128, 8], F32))
        ubias_s = E(nc.sbuf_tensor("ubias_s", [128, NU * 8], F32))
        fc2bT_s = E(nc.sbuf_tensor("fc2bT_s", [8, D], BF16))
        sh2b_s = E(nc.sbuf_tensor("sh2b_s", [1, D], BF16))
        ecol_s = E(nc.sbuf_tensor("ecol_s", [128, 1], F32))
        esel_s = E(nc.sbuf_tensor("esel_s", [128, 8], F32))
        eselT_s = E(nc.sbuf_tensor("eselT_s", [8, 128], F32))
        iota8_s = E(nc.sbuf_tensor("iota8_s", [128, 8], F32))
        ident8_s = E(nc.sbuf_tensor("ident8_s", [8, 8], F32))
        ones8x128_s = E(nc.sbuf_tensor("ones8x128_s", [8, 128], F32))
        sel_s = E(nc.sbuf_tensor("sel_s", [8, 8 * 128], BF16))
        inb_s = E(nc.sbuf_tensor("inb_s", [128, 8], F32))
        modb1_s = E(nc.sbuf_tensor("modb1_s", [128, 32], F32))
        modb2_s = E(nc.sbuf_tensor("modb2_s", [128, 8], F32))
        outb_s = E(nc.sbuf_tensor("outb_s", [128, 8], F32))
        kern_s = x1t[:, 512:1024]
        stats = E(nc.sbuf_tensor("stats", [128, 2 * 12], F32))
        mv = E(nc.sbuf_tensor("mv", [128, 2 * 2], F32))
        veps = E(nc.sbuf_tensor("veps", [128, 2], F32))
        rstd = E(nc.sbuf_tensor("rstd", [128, 2], F32))
        wtmp = x2t[:, 1184:1312]
        getmp = x2t[:, 1312:1440]
        m8 = E(nc.sbuf_tensor("m8", [128, 8], F32))
        i8 = E(nc.sbuf_tensor("i8", [128, 8], U32))
        rtmp = E(nc.sbuf_tensor("rtmp", [128, 16], F32))
        lohi = E(nc.sbuf_tensor("lohi", [8, 8], F32))
        cnt_col = E(nc.sbuf_tensor("cnt_col", [128, 1], F32))
        lo_bc = E(nc.sbuf_tensor("lo_bc", [128, 8], F32))
        X8 = E(nc.sbuf_tensor("X8", [8, 8], F32))
        gtm = E(nc.sbuf_tensor("gtm", [128, 8], BF16))
        geq = E(nc.sbuf_tensor("geq", [128, 2 * 8], F32))
        if exchange == "shared":
            nt_s = E(nc.sbuf_tensor("nt_s", [1, 4], U32))
            ft_s = E(nc.sbuf_tensor("ft_s", [1, 128], U32))

        ps0 = E(nc.psum_tensor("ps0", [128, 512], F32))
        ps1 = E(nc.psum_tensor("ps1", [128, 512], F32))
        psY0 = E(nc.psum_tensor("psY0", [128, 512], F32))
        psY1 = E(nc.psum_tensor("psY1", [128, 512], F32))
        psT0 = E(nc.psum_tensor("psT0", [128, 1024], BF16))
        psT1 = E(nc.psum_tensor("psT1", [128, 1024], BF16))
        psZ = E(nc.psum_tensor("psZ", [128, 512], F32))
        psB = E(nc.psum_tensor("psB", [128, 512], F32))
        psA0, psA1 = psY0, psY1

        def S(name):
            return Sem(E(nc.semaphore(name)))

        d_pre = S("d_pre")
        d_x = [S("d_x0"), S("d_x1")]
        d_o = [S("d_o0"), S("d_o1")]
        d_z = S("d_z")
        d_sh = S("d_sh")
        d_gr = S("d_gr")
        d_w1 = [S("d_w1a"), S("d_w1b")]
        d_w2 = [S("d_w2a"), S("d_w2b")]
        d_wm = [S("d_wm0"), S("d_wm1")]
        d_wf = S("d_wf")
        t_tr = S("t_tr")
        t_s = S("t_s")
        t_m2 = S("t_m2")
        t_z = S("t_z")
        t_mm = S("t_mm")
        t_y = S("t_y")
        t_cnt = S("t_cnt")
        t_thr = S("t_thr")
        t_x8 = S("t_x8")
        t_cv = S("t_cv")
        v_st = S("v_st")
        s_rs = S("s_rs")
        v_nrm = S("v_nrm")
        v_ev = S("v_ev")
        v_w = S("v_w")
        v_x1 = S("v_x1")
        v_ln = S("v_ln")
        v_z = S("v_z")
        s_w = S("s_w")
        v_pw = S("v_pw")
        v_zb = S("v_zb")
        s_sg = S("s_sg")
        v_gd = S("v_gd")
        v_it = S("v_it")
        v_gt = S("v_gt")
        v_g8 = S("v_g8")
        v_gc = S("v_gc")
        v_h = S("v_h")
        s_h = S("s_h")
        v_y = S("v_y")
        v_x2 = S("v_x2")
        v_diag = S("v_diag")
        v_cs = S("v_cs")
        v_mo = S("v_mo")
        v_ps = S("v_ps")
        v_x3 = S("v_x3")
        chV = E(nc.semaphore("chV"))
        chS = E(nc.semaphore("chS"))

        OPS = {e: [] for e in ("sync", "gpsimd", "tensor", "vector", "scalar")}
        PHB = []

        def phase(name):
            PHB.append((name, {e: len(OPS[e]) for e in OPS}))

        def op(eng, fn):
            OPS[eng].append(fn)

        rotA = Rot([ps0, ps1])
        rotY = Rot([psY0, psY1])
        rotT = Rot([psT0, psT1])

        def wait_guard(e, guard):
            if guard is not None:
                e.wait_ge(guard[0].h, guard[1])

        def arms(g):
            if cora is not None:
                yield cora
            else:
                for arm in nc.switch_core_id(engines=[g], n=128):
                    yield arm.logical % 8

        pre_list = [
            (identt[:], identb), (Rm_s[:], RmT), (Dm_s[:], DmT), (ones_s[:], onesT),
            (scal_s[:], scal), (qs_s[:], qs_cm), (ks_s[:], ks_cm), (projT_s[:], projT),
            (pb_s[:], pb8), (eb_s[:], eb8), (ubias_s[:], ubias), (fc2bT_s[:], fc2bT),
            (sh2b_s[:], sh2b_row), (ecol_s[:], e_colC), (esel_s[:], EselC),
            (eselT_s[:], EselTC), (ones512_s[:], ones512C), (iota8_s[:], iota8C),
            (ident8_s[:], ident8C), (ones8x128_s[:], ones8x128C),
            (inb_s[:], in_b_cm), (modb1_s[:], mod_b1_cm), (modb2_s[:], mod_b2_cm),
            (outb_s[:], out_b_cm),
        ] + [(sel_s[:, 128 * e:128 * (e + 1)], selC[e]) for e in range(8)]
        if exchange == "shared":
            pre_list.append((nt_s[:], nonce[:, 0:4]))

        for dst, src in pre_list:
            d_pre.bump(16)

            def f(g, dst=dst, src=src):
                g.dma_start(out=dst, in_=src).then_inc(d_pre.h, 16)
            op("gpsimd", f)
        NPRE = d_pre.n

        def ftp(t):
            t.wait_ge(d_pre.h, NPRE)
            t.matmul(psB[:, 0:4], ones_s[:], scal_s[:], start=True,
                     stop=True).then_inc(t_s.h, 1)
        op("tensor", ftp)
        t_s.bump()

        def fvp(v):
            v.wait_ge(t_s.h, 1)
            v.tensor_copy(out=scalc[:], in_=psB[:, 0:4])
            v.mark(v_pw.h)
        op("vector", fvp)
        v_pw.bump()

        def fsp(s):
            s.wait_ge(v_pw.h, 1)
            s.activation(out=sigd[:], in_=scalc[:, 0:1], func=AF.Sigmoid)
            s.mark(s_w.h)
        op("scalar", fsp)
        s_w.bump()

        def fvp2(v):
            v.wait_ge(s_w.h, 1)
            v.tensor_scalar(out=getmp[:, 0:1], in0=sigd[:], scalar1=1e-38, scalar2=None,
                            op0=ALU.max)
            v.mark(v_pw.h)
        op("vector", fvp2)
        v_pw.bump()

        def fsp2(s):
            s.wait_ge(v_pw.h, 2)
            s.activation(out=lnd[:], in_=getmp[:, 0:1], func=AF.Ln)
            s.mark(s_w.h)
        op("scalar", fsp2)
        s_w.bump()

        def fvp3(v):
            v.wait_ge(s_w.h, 2)
            v.tensor_scalar(out=Rl[:], in0=Rm_s[:], scalar1=lnd[:], scalar2=None, op0=ALU.mult)
            v.tensor_scalar(out=Dl[:], in0=Dm_s[:], scalar1=lnd[:], scalar2=None, op0=ALU.mult)
        op("vector", fvp3)

        for m in range(32):
            npw = v_pw.bump()

            def fvw(v, m=m, nsw_prev=s_w.n):
                if m >= 1:
                    v.wait_ge(s_w.h, nsw_prev)
                v.tensor_scalar(out=wtmp[:], in0=Rl[:], scalar1=Dl[:, m:m + 1],
                                scalar2=0.0, op0=ALU.add, op1=ALU.min)
                v.mark(v_pw.h)
            op("vector", fvw)
            nsw = s_w.bump()

            def fsw(s, m=m, npw=npw):
                s.wait_ge(v_pw.h, npw)
                s.activation(out=Wp[:, 128 * m:128 * (m + 1)], in_=wtmp[:], func=AF.Exp)
                s.mark(s_w.h)
            op("scalar", fsw)

        npw_mask = v_pw.bump()

        def fvmask(v, nsw=s_w.n):
            v.wait_ge(s_w.h, nsw)
            v.tensor_scalar(out=getmp[:], in0=Rl[:], scalar1=Dl[:, 0:1],
                            scalar2=0.0, op0=ALU.add, op1=ALU.is_le)
            v.tensor_mul(out=Wp[:, 0:128], in0=Wp[:, 0:128], in1=getmp[:])
            v.mark(v_pw.h)
        op("vector", fvmask)
        NPW = v_pw.n

        def emit_attn_half(xrow_of, TS, zcol_of):
            TQ = 16
            pairs = [(q, s) for q in range(TQ) for s in range(q, TS)]
            first_idx = {}
            last_idx = {}
            for i, (q, s) in enumerate(pairs):
                if s == q:
                    first_idx[q] = i
                if s == TS - 1:
                    last_idx[q] = i
            VX1_PREV = v_x1.n

            for i in range(TS):
                par = i % 2
                dsem = d_x[par]
                nld = dsem.bump(16)

                def fs(sy, i=i, par=par, dsem=dsem, nld=nld, nnrm_prev=v_nrm.n,
                       row=xrow_of(i), first=(i == 0), vx1p=VX1_PREV):
                    if first:
                        sy.wait_ge(d_pre.h, NPRE)
                    if i < 2 and vx1p >= 1:
                        sy.wait_ge(v_x1.h, vx1p)
                    if nnrm_prev >= 2:
                        sy.wait_ge(v_nrm.h, nnrm_prev - 1)
                    sy.dma_start(out=xbuf[:, par * D:(par + 1) * D],
                                 in_=xin[row:row + 128, :]).then_inc(dsem.h, 16)
                op("sync", fs)

                nst = v_st.bump()

                def fv1(v, par=par, nld=nld, dsem=dsem):
                    v.wait_ge(dsem.h, nld)
                    xb = xbuf[:, par * D:(par + 1) * D]
                    st = stats[:, par * 12:par * 12 + 12]
                    v.bn_stats(out=st[:, 0:6], in_=xb[:, 0:512])
                    v.bn_stats(out=st[:, 6:12], in_=xb[:, 512:1024])
                    v.bn_aggr(out=mv[:, par * 2:par * 2 + 2], in_=st)
                    v.tensor_scalar(out=veps[:, par:par + 1],
                                    in0=mv[:, par * 2 + 1:par * 2 + 2],
                                    scalar1=1e-5, scalar2=None, op0=ALU.add)
                    v.mark(v_st.h)
                op("vector", fv1)

                nrs = s_rs.bump()

                def fsc(s, par=par, nst=nst, nnrm_prev=v_nrm.n):
                    s.wait_ge(v_st.h, nst)
                    if nnrm_prev >= 1:
                        s.wait_ge(v_nrm.h, nnrm_prev)
                    s.activation(out=rstd[:, par:par + 1], in_=veps[:, par:par + 1], func=AF.Ln)
                    s.activation(out=rstd[:, par:par + 1], in_=rstd[:, par:par + 1],
                                 func=AF.Exp, scale=-0.5)
                    s.mark(s_rs.h)
                op("scalar", fsc)

                nnrm = v_nrm.bump()

                def fv2(v, i=i, par=par, nrs=nrs):
                    v.wait_ge(s_rs.h, nrs)
                    xb = xbuf[:, par * D:(par + 1) * D]
                    v.tensor_scalar(out=V[:, D * i:D * (i + 1)], in0=xb,
                                    scalar1=mv[:, par * 2:par * 2 + 1],
                                    scalar2=rstd[:, par:par + 1],
                                    op0=ALU.subtract, op1=ALU.mult)
                    v.mark(v_nrm.h)
                op("vector", fv2)

                for ct in range(NCT):
                    ps, guard, g = rotT.acquire()
                    ntr = t_tr.bump()

                    def ft(t, i=i, ct=ct, ps=ps, guard=guard, nnrm=nnrm):
                        if ct == 0:
                            t.wait_ge(v_nrm.h, nnrm)
                        wait_guard(t, guard)
                        t.transpose(ps[:, 0:128], V[:, D * i + 128 * ct: D * i + 128 * (ct + 1)],
                                    identt[:]).then_inc(t_tr.h, 1)
                    op("tensor", ft)

                    nev = v_ev.bump()

                    def fv3(v, i=i, ct=ct, ps=ps, ntr=ntr):
                        v.wait_ge(t_tr.h, ntr)
                        v.tensor_scalar(out=K[:, 4096 * ct + 128 * i: 4096 * ct + 128 * (i + 1)],
                                        in0=ps[:, 0:128], scalar1=ks_s[:, ct:ct + 1],
                                        scalar2=None, op0=ALU.mult)
                        v.mark(v_ev.h)
                    op("vector", fv3)
                    rotT.release(g, v_ev, nev)

            for q in range(TQ):
                par = q % 2
                dsem = d_x[par]
                nld0 = dsem.bump(16)

                def fsq(sy, par=par, dsem=dsem, row=xrow_of(q), nx1_prev=v_x1.n,
                        q=q, nrm_all=v_nrm.n):
                    if q < 2:
                        sy.wait_ge(v_nrm.h, nrm_all)
                    else:
                        sy.wait_ge(v_x1.h, nx1_prev - 1)
                    sy.dma_start(out=xbuf[:, par * D:(par + 1) * D],
                                 in_=xin[row:row + 128, :]).then_inc(dsem.h, 16)
                op("sync", fsq)

                for ct in range(NCT):
                    ps, guard, g = rotT.acquire()
                    ntr = t_tr.bump()

                    def ftq(t, q=q, ct=ct, ps=ps, guard=guard):
                        wait_guard(t, guard)
                        t.transpose(ps[:, 0:128], V[:, D * q + 128 * ct: D * q + 128 * (ct + 1)],
                                    identt[:]).then_inc(t_tr.h, 1)
                    op("tensor", ftq)

                    nev = v_ev.bump()

                    def fvq(v, ct=ct, ps=ps, ntr=ntr, npw=NPW):
                        v.wait_ge(t_tr.h, ntr)
                        if ct == 0:
                            v.wait_ge(v_pw.h, npw)
                        v.tensor_scalar(out=Qt[:, 128 * ct:128 * (ct + 1)],
                                        in0=ps[:, 0:128], scalar1=qs_s[:, ct:ct + 1],
                                        scalar2=None, op0=ALU.mult)
                        v.mark(v_ev.h)
                    op("vector", fvq)
                    rotT.release(g, v_ev, nev)
                QEV = v_ev.n

                mm2_pend = []
                for idx in range(first_idx[q], last_idx[q] + 1):
                    (qq, s) = pairs[idx]
                    fi = (s == q)
                    la = (s == TS - 1)
                    slot = idx % 2
                    nts = t_s.bump()

                    def fts(t, q=q, s=s, slot=slot, fi=fi, qev=QEV,
                            nvw_prev=v_w.n):
                        if fi:
                            t.wait_ge(v_ev.h, qev)
                        if nvw_prev >= 2:
                            t.wait_ge(v_w.h, nvw_prev - 1)
                        psS = ps0 if slot == 0 else ps1
                        for ct in range(NCT):
                            mi = t.matmul(psS[:, 0:128],
                                          K[:, 4096 * ct + 128 * s: 4096 * ct + 128 * (s + 1)],
                                          Qt[:, 128 * ct:128 * (ct + 1)],
                                          start=(ct == 0), stop=(ct == NCT - 1))
                        mi.then_inc(t_s.h, 1)
                    op("tensor", fts)

                    nvw = v_w.bump()

                    def fvw2(v, q=q, s=s, slot=slot, nts=nts, nm2_prev=t_m2.n):
                        v.wait_ge(t_s.h, nts)
                        if nm2_prev >= 2:
                            v.wait_ge(t_m2.h, nm2_prev - 1)
                        psS = ps0 if slot == 0 else ps1
                        m = s - q
                        v.tensor_mul(out=sw[:, slot * 128:(slot + 1) * 128],
                                     in0=psS[:, 0:128], in1=Wp[:, 128 * m:128 * (m + 1)])
                        v.mark(v_w.h)
                    op("vector", fvw2)

                    nm2 = t_m2.bump()

                    def ftm2(t, s=s, slot=slot, fi=fi, la=la, nvw=nvw,
                             nx1_prev=v_x1.n):
                        t.wait_ge(v_w.h, nvw)
                        if fi and nx1_prev >= 1:
                            t.wait_ge(v_x1.h, nx1_prev)
                        swk = sw[:, slot * 128:(slot + 1) * 128]
                        t.matmul(psA0[:], swk, V[:, D * s: D * s + 512],
                                 start=fi, stop=la)
                        t.matmul(psA1[:], swk, V[:, D * s + 512: D * s + 1024],
                                 start=fi, stop=la).then_inc(t_m2.h, 1)
                    op("tensor", ftm2)

                nx1 = v_x1.bump()

                def fvx1(v, par=par, nm2=t_m2.n, nld0=nld0, dsem=dsem,
                         ndo_prev=d_o[par].n):
                    v.wait_ge(t_m2.h, nm2)
                    v.wait_ge(dsem.h, nld0)
                    if ndo_prev >= 16:
                        v.wait_ge(d_o[par].h, ndo_prev)
                    x0 = xbuf[:, par * D:(par + 1) * D]
                    xt = x1t[:, par * D:(par + 1) * D]
                    v.scalar_tensor_tensor(out=xt[:, 0:512], in0=psA0[:],
                                           scalar=scalc[:, 1:2], in1=x0[:, 0:512],
                                           op0=ALU.mult, op1=ALU.add)
                    v.scalar_tensor_tensor(out=xt[:, 512:1024], in0=psA1[:],
                                           scalar=scalc[:, 1:2], in1=x0[:, 512:1024],
                                           op0=ALU.mult, op1=ALU.add)
                    v.mark(v_x1.h)
                op("vector", fvx1)

                ndo = d_o[par].bump(16)

                def fgo(g, par=par, nx1=nx1, row=xrow_of(q)):
                    g.wait_ge(v_x1.h, nx1)
                    g.dma_start(out=x1d[row:row + 128, :],
                                in_=x1t[:, par * D:(par + 1) * D]).then_inc(d_o[par].h, 16)
                op("gpsimd", fgo)

                nst = v_st.bump()

                def fvl(v, par=par):
                    xt = x1t[:, par * D:(par + 1) * D]
                    st = stats[:, par * 12:par * 12 + 12]
                    v.bn_stats(out=st[:, 0:6], in_=xt[:, 0:512])
                    v.bn_stats(out=st[:, 6:12], in_=xt[:, 512:1024])
                    v.bn_aggr(out=mv[:, par * 2:par * 2 + 2], in_=st)
                    v.tensor_scalar(out=veps[:, par:par + 1],
                                    in0=mv[:, par * 2 + 1:par * 2 + 2],
                                    scalar1=1e-5, scalar2=None, op0=ALU.add)
                    v.mark(v_st.h)
                op("vector", fvl)

                nrs = s_rs.bump()

                def fscl(s, par=par, nst=nst, nln_prev=v_ln.n):
                    s.wait_ge(v_st.h, nst)
                    if nln_prev >= 1:
                        s.wait_ge(v_ln.h, nln_prev)
                    s.activation(out=rstd[:, par:par + 1], in_=veps[:, par:par + 1], func=AF.Ln)
                    s.activation(out=rstd[:, par:par + 1], in_=rstd[:, par:par + 1],
                                 func=AF.Exp, scale=-0.5)
                    s.mark(s_rs.h)
                op("scalar", fscl)

                tr0 = t_tr.n
                nln = v_ln.bump()

                def fvn(v, par=par, nrs=nrs, tr0=tr0):
                    v.wait_ge(s_rs.h, nrs)
                    if tr0 > 0:
                        v.wait_ge(t_tr.h, tr0)
                    xt = x1t[:, par * D:(par + 1) * D]
                    v.tensor_scalar(out=lnt[:], in0=xt,
                                    scalar1=mv[:, par * 2:par * 2 + 1],
                                    scalar2=rstd[:, par:par + 1],
                                    op0=ALU.subtract, op1=ALU.mult)
                    v.mark(v_ln.h)
                op("vector", fvn)

                for ct in range(NCT):
                    ps, guard, g = rotT.acquire()
                    ntr = t_tr.bump()

                    def ftz(t, ct=ct, ps=ps, guard=guard, nln=nln):
                        if ct == 0:
                            t.wait_ge(v_ln.h, nln)
                        wait_guard(t, guard)
                        t.transpose(ps[:, 0:128], lnt[:, 128 * ct:128 * (ct + 1)],
                                    identt[:]).then_inc(t_tr.h, 1)
                    op("tensor", ftz)

                    nev = v_ev.bump()

                    def fvze(v, ct=ct, ps=ps, ntr=ntr):
                        v.wait_ge(t_tr.h, ntr)
                        v.tensor_copy(out=lnzT[:, 128 * ct:128 * (ct + 1)], in_=ps[:, 0:128])
                        v.mark(v_ev.h)
                    op("vector", fvze)
                    rotT.release(g, v_ev, nev)
                EVZ = v_ev.n

                ntz = t_z.bump()

                def ftzm(t, evz=EVZ, nz_prev=v_z.n):
                    t.wait_ge(v_ev.h, evz)
                    if nz_prev >= 1:
                        t.wait_ge(v_z.h, nz_prev)
                    for ct in range(NCT):
                        mi = t.matmul(psZ[:, 0:8], lnzT[:, 128 * ct:128 * (ct + 1)],
                                      projT_s[:, 8 * ct:8 * (ct + 1)],
                                      start=(ct == 0), stop=(ct == NCT - 1))
                    mi.then_inc(t_z.h, 1)
                op("tensor", ftzm)

                nvz = v_z.bump()

                def fvzz(v, zc=zcol_of(q), ntz=ntz):
                    v.wait_ge(t_z.h, ntz)
                    v.tensor_copy(out=z_sb[:, 8 * zc:8 * (zc + 1)], in_=psZ[:, 0:8])
                    v.mark(v_z.h)
                op("vector", fvzz)

        for bi in range(2):
            emit_attn_half(lambda i, bi=bi: 4096 * bi + 128 * i, 32,
                           lambda q, bi=bi: 32 * bi + q)
            emit_attn_half(lambda i, bi=bi: 4096 * bi + 2048 + 128 * i, 16,
                           lambda q, bi=bi: 32 * bi + 16 + q)
        NVZ_R0 = v_z.n

        d_po = [S("d_po0"), S("d_po1")]

        def rest_runs(T0):
            if T0 == 0:
                return [(16, 40), (40, 64)]
            if T0 == 48:
                return [(0, 24), (24, 48)]
            return [(0, T0), (T0 + 16, 64)]

        def emit_exchange(rnd, vz_need):
            nsh0 = d_sh.n
            if exchange == "shared":
                d_sh.bump(32)

            def fex(g, rnd=rnd, vz_need=vz_need, nsh0=nsh0,
                    do0=d_o[0].n, do1=d_o[1].n):
                g.wait_ge(v_z.h, vz_need)
                if do0:
                    g.wait_ge(d_o[0].h, do0)
                if do1:
                    g.wait_ge(d_o[1].h, do1)
                if exchange == "shared":
                    for c in arms(g):
                        p = c % 2
                        g.dma_start(out=shz[128 * p:128 * (p + 1), 512 * rnd:512 * (rnd + 1)],
                                    in_=z_sb[:]).then_inc(d_sh.h, 16)
                    g.wait_ge(d_sh.h, nsh0 + 16)
                    for c in arms(g):
                        p = c % 2
                        g.dma_start(out=shflag[p:p + 1, rnd:rnd + 1],
                                    in_=nt_s[:, 0:1]).then_inc(d_sh.h, 16)
                    g.wait_ge(d_sh.h, nsh0 + 32)
                    with g.register(f"pc{rnd}") as creg, g.register(f"pw{rnd}") as wreg:
                        g.reg_mov(creg, 1)
                        g.reg_mov(wreg, 16)
                        with g.While(creg):
                            for c in arms(g):
                                qpar = 1 - (c % 2)
                                g.dma_start(out=ft_s[:, 0:1],
                                            in_=shflag[qpar:qpar + 1, rnd:rnd + 1]
                                            ).then_inc(d_po[rnd].h, 16)
                            g.wait_ge(d_po[rnd].h, wreg)
                            g.reg_alu(wreg, wreg, 16, op=ALU.add)
                            vv = g.value_load(ft_s[0:1, 0:1])
                            nn = g.value_load(nt_s[0:1, 0:1])
                            g.reg_alu(creg, vv, nn, op=ALU.not_equal)
                    for c in arms(g):
                        qpar = 1 - (c % 2)
                        g.dma_start(out=zsb[:, 512:1024],
                                    in_=shz[128 * qpar:128 * (qpar + 1),
                                            512 * rnd:512 * (rnd + 1)]).then_inc(d_z.h, 16)
                else:
                    src = pz0 if rnd == 0 else pz1
                    g.dma_start(out=zsb[:, 512:1024], in_=src).then_inc(d_z.h, 16)
                if rnd == 0:
                    g.dma_start(out=zsb[:, 0:512], in_=z_sb[:]).then_inc(d_z.h, 16)
                else:
                    for c in arms(g):
                        T0 = core_T0(c)
                        g.dma_start(out=zsb[:, 0:128],
                                    in_=z_sb[:, 8 * T0:8 * T0 + 128]).then_inc(d_z.h, 16)
                    for run in range(2):
                        for c in arms(g):
                            T0 = core_T0(c)
                            runs = rest_runs(T0)
                            off = 128 + 8 * (runs[0][1] - runs[0][0]) * run
                            a, b = runs[run]
                            g.dma_start(out=zsb[:, off:off + 8 * (b - a)],
                                        in_=z_sb[:, 8 * a:8 * b]).then_inc(d_z.h, 16)
            op("gpsimd", fex)
            nblocks = 2 if rnd == 0 else 4
            return d_z.bump(16 * nblocks)

        def emit_router(xr_need):
            nzb = v_zb.bump()

            def fvr(v, xr_need=xr_need):
                v.wait_ge(d_z.h, xr_need)
                for zt in range(NZT):
                    v.tensor_add(out=sall[:, 8 * zt:8 * (zt + 1)],
                                 in0=zsb[:, 8 * zt:8 * (zt + 1)], in1=pb_s[:])
                v.mark(v_zb.h)
            op("vector", fvr)
            nsg = s_sg.bump()

            def fsr(s, nzb=nzb):
                s.wait_ge(v_zb.h, nzb)
                s.activation(out=zsb[:], in_=sall[:], func=AF.Sigmoid)
                s.mark(s_sg.h)
            op("scalar", fsr)

            def fvr2(v, nsg=nsg):
                v.wait_ge(s_sg.h, nsg)
                for zt in range(NZT):
                    v.tensor_add(out=sall[:, 8 * zt:8 * (zt + 1)],
                                 in0=zsb[:, 8 * zt:8 * (zt + 1)], in1=eb_s[:])
                v.tensor_scalar(out=sall[:], in0=sall[:], scalar1=1e-12, scalar2=None,
                                op0=ALU.max)
            op("vector", fvr2)

            for zt in range(NZT):
                def fvr3(v, zt=zt):
                    sl = sall[:, 8 * zt:8 * (zt + 1)]
                    v.max(out=m8[:], in_=sl)
                    v.max_index(out=i8[:], in_max=m8[:], in_values=sl)
                    v.tensor_copy(out=egrid[:, 2 * zt:2 * zt + 2], in_=i8[:, 0:2])
                    v.tensor_add(out=rtmp[:, 0:1], in0=m8[:, 0:1], in1=m8[:, 1:2])
                    v.tensor_scalar(out=rtmp[:, 1:2], in0=rtmp[:, 0:1], scalar1=1e-12,
                                    scalar2=None, op0=ALU.max)
                    v.reciprocal(out=rtmp[:, 2:3], in_=rtmp[:, 1:2])
                    v.tensor_mul(out=wgrid[:, 2 * zt:2 * zt + 1], in0=m8[:, 0:1],
                                 in1=rtmp[:, 2:3])
                    v.tensor_mul(out=wgrid[:, 2 * zt + 1:2 * zt + 2], in0=m8[:, 1:2],
                                 in1=rtmp[:, 2:3])
                op("vector", fvr3)
            ngd = v_gd.bump()

            def fvr4(v):
                v.mark(v_gd.h)
            op("vector", fvr4)
            return v_gd.n

        def emit_bisect(ngd):
            n_gr1 = d_gr.bump(32)

            def fgb(g, ngd=ngd, n_gr0=n_gr1 - 32):
                g.wait_ge(v_gd.h, ngd)
                g.dma_start(out=wdram, in_=wgrid[:]).then_inc(d_gr.h, 16)
                g.dma_start(out=edram, in_=egrid[:]).then_inc(d_gr.h, 16)
            op("gpsimd", fgb)
            n_gr2 = d_gr.bump(16 * 16)

            def fgb2(g, n_gr1=n_gr1, do0=d_o[0].n, do1=d_o[1].n,
                     dx0=d_x[0].n, dx1=d_x[1].n):
                g.wait_ge(d_gr.h, n_gr1)
                if do0:
                    g.wait_ge(d_o[0].h, do0)
                if do1:
                    g.wait_ge(d_o[1].h, do1)
                if dx0:
                    g.wait_ge(d_x[0].h, dx0)
                if dx1:
                    g.wait_ge(d_x[1].h, dx1)
                for e in range(8):
                    g.dma_start(out=wge[16 * e:16 * (e + 1), 0:2048],
                                in_=wdram.rearrange("(a b) f -> a (b f)", a=16)
                                ).then_inc(d_gr.h, 16)
                    g.dma_start(out=epe[16 * e:16 * (e + 1), 0:2048],
                                in_=edram.rearrange("(a b) f -> a (b f)", a=16)
                                ).then_inc(d_gr.h, 16)
            op("gpsimd", fgb2)

            nit0 = v_it.bump()

            def fvb(v, n_gr2=n_gr2):
                v.wait_ge(d_gr.h, n_gr2)
                v.tensor_scalar(out=epe[:, 0:2048], in0=epe[:, 0:2048], scalar1=ecol_s[:],
                                scalar2=None, op0=ALU.is_equal)
                v.tensor_scalar(out=wge[:, 0:2048], in0=wge[:, 0:2048], scalar1=1.0,
                                scalar2=None, op0=ALU.add)
                v.tensor_mul(out=wge[:, 0:2048], in0=wge[:, 0:2048], in1=epe[:, 0:2048])
                v.tensor_scalar(out=wge[:, 0:2048], in0=wge[:, 0:2048], scalar1=1.0,
                                scalar2=None, op0=ALU.subtract)
                v.memset(lohi[:, 0:8], 0.0)
                v.memset(lohi[:, 2:3], 1.0)
                v.memset(lohi[:, 4:5], 0.5)
                v.mark(v_it.h)
            op("vector", fvb)

            for k in range(NITER):
                nthr = t_thr.bump()

                def ftb1(t, req=v_it.n):
                    t.wait_ge(v_it.h, req)
                    t.matmul(psB[:, 0:1], eselT_s[:], lohi[:, 4:5],
                             start=True, stop=True).then_inc(t_thr.h, 1)
                op("tensor", ftb1)

                nv1 = v_it.bump()

                def fvb1(v, nthr=nthr):
                    v.wait_ge(t_thr.h, nthr)
                    v.tensor_scalar(out=epe[:, 0:2048], in0=wge[:, 0:2048],
                                    scalar1=psB[:, 0:1], scalar2=None,
                                    op0=ALU.is_gt, op1=ALU.add, accum_out=cnt_col[:])
                    v.mark(v_it.h)
                op("vector", fvb1)

                ncnt = t_cnt.bump()

                def ftb2(t, nv1=nv1):
                    t.wait_ge(v_it.h, nv1)
                    t.matmul(psB[0:8, 0:1], esel_s[:], cnt_col[:],
                             start=True, stop=True).then_inc(t_cnt.h, 1)
                op("tensor", ftb2)

                nv2 = v_it.bump()

                def fvb2(v, k=k, ncnt=ncnt):
                    v.wait_ge(t_cnt.h, ncnt)
                    lo = lohi[:, k % 2:k % 2 + 1]
                    lon = lohi[:, (k + 1) % 2:(k + 1) % 2 + 1]
                    hi = lohi[:, 2 + k % 2:3 + k % 2]
                    hin = lohi[:, 2 + (k + 1) % 2:3 + (k + 1) % 2]
                    ge = lohi[:, 5:6]
                    gn = lohi[:, 6:7]
                    tmp = lohi[:, 7:8]
                    mid = lohi[:, 4:5]
                    v.tensor_scalar(out=ge, in0=psB[0:8, 0:1], scalar1=CAP,
                                    scalar2=None, op0=ALU.is_ge)
                    v.tensor_scalar(out=gn, in0=ge, scalar1=-1.0, scalar2=1.0,
                                    op0=ALU.mult, op1=ALU.add)
                    v.tensor_mul(out=tmp, in0=ge, in1=mid)
                    v.tensor_mul(out=lon, in0=gn, in1=lo)
                    v.tensor_add(out=lon, in0=lon, in1=tmp)
                    v.tensor_mul(out=tmp, in0=ge, in1=hi)
                    v.tensor_mul(out=hin, in0=gn, in1=mid)
                    v.tensor_add(out=hin, in0=hin, in1=tmp)
                    v.tensor_add(out=tmp, in0=lon, in1=hin)
                    v.tensor_scalar(out=mid, in0=tmp, scalar1=0.5, scalar2=None,
                                    op0=ALU.mult)
                    v.mark(v_it.h)
                op("vector", fvb2)

            fin = NITER % 2
            nx8v = v_it.bump()

            def fvx(v, fin=fin):
                v.tensor_scalar(out=X8[:], in0=ident8_s[:], scalar1=lohi[:, fin:fin + 1],
                                scalar2=None, op0=ALU.mult)
                v.mark(v_it.h)
            op("vector", fvx)
            nx8 = t_x8.bump()

            def ftx(t, nx8v=nx8v):
                t.wait_ge(v_it.h, nx8v)
                t.matmul(psB[:, 0:8], ones8x128_s[:], X8[:],
                         start=True, stop=True).then_inc(t_x8.h, 1)
            op("tensor", ftx)

            nfin = v_it.bump()

            def fvx2(v, nx8=nx8):
                v.wait_ge(t_x8.h, nx8)
                v.tensor_copy(out=lo_bc[:], in_=psB[:, 0:8])
                v.mark(v_it.h)
            op("vector", fvx2)
            return v_it.n

        def emit_gates(ntiles, tiles_off=0):
            for k in range(ntiles):
                ti = tiles_off + k
                ngt = v_gt.bump()

                def fvg(v, ti=ti):
                    v.tensor_scalar(out=geq[:, 0:8], in0=iota8_s[:],
                                    scalar1=egrid[:, 2 * ti:2 * ti + 1],
                                    scalar2=None, op0=ALU.is_equal)
                    v.tensor_scalar(out=geq[:, 8:16], in0=iota8_s[:],
                                    scalar1=egrid[:, 2 * ti + 1:2 * ti + 2],
                                    scalar2=None, op0=ALU.is_equal)
                    v.tensor_mul(out=rtmp[:, 8:16], in0=geq[:, 0:8], in1=lo_bc[:])
                    v.reduce_sum(out=rtmp[:, 0:1], in_=rtmp[:, 8:16], axis=AX.X)
                    v.tensor_mul(out=rtmp[:, 8:16], in0=geq[:, 8:16], in1=lo_bc[:])
                    v.reduce_sum(out=rtmp[:, 1:2], in_=rtmp[:, 8:16], axis=AX.X)
                    v.tensor_tensor(out=rtmp[:, 2:3], in0=wgrid[:, 2 * ti:2 * ti + 1],
                                    in1=rtmp[:, 0:1], op=ALU.is_gt)
                    v.tensor_tensor(out=rtmp[:, 3:4], in0=wgrid[:, 2 * ti + 1:2 * ti + 2],
                                    in1=rtmp[:, 1:2], op=ALU.is_gt)
                    v.tensor_mul(out=rtmp[:, 4:5], in0=wgrid[:, 2 * ti:2 * ti + 1],
                                 in1=rtmp[:, 2:3])
                    v.tensor_mul(out=rtmp[:, 5:6], in0=wgrid[:, 2 * ti + 1:2 * ti + 2],
                                 in1=rtmp[:, 3:4])
                    v.tensor_scalar(out=geq[:, 0:8], in0=geq[:, 0:8], scalar1=rtmp[:, 4:5],
                                    scalar2=None, op0=ALU.mult)
                    v.tensor_scalar(out=geq[:, 8:16], in0=geq[:, 8:16], scalar1=rtmp[:, 5:6],
                                    scalar2=None, op0=ALU.mult)
                    v.tensor_add(out=gtm[:], in0=geq[:, 0:8], in1=geq[:, 8:16])
                    v.mark(v_gt.h)
                op("vector", fvg)

                ps, guard, g = rotT.acquire()
                ntr = t_tr.bump()

                def ftg(t, ps=ps, guard=guard, ngt=ngt):
                    t.wait_ge(v_gt.h, ngt)
                    wait_guard(t, guard)
                    t.transpose(ps[0:8, 0:128], gtm[:], identt[:]).then_inc(t_tr.h, 1)
                op("tensor", ftg)

                ng8 = v_g8.bump()

                def fvg8(v, k=k, ps=ps, ntr=ntr):
                    v.wait_ge(t_tr.h, ntr)
                    v.tensor_copy(out=g8all[:, 128 * k:128 * (k + 1)], in_=ps[0:8, 0:128])
                    v.mark(v_g8.h)
                op("vector", fvg8)
                rotT.release(g, v_g8, ng8)
            return v_g8.n

        wslot_guard = {}
        hm_guard = [None]

        def emit_ln_tile(row_fn, cm_dst_col, first_waits):
            ti_par = emit_ln_tile.par
            emit_ln_tile.par ^= 1
            par = ti_par
            dsem = d_x[par]
            nld = dsem.bump(16)

            def fs(sy, par=par, dsem=dsem, row_fn=row_fn, fw=first_waits,
                   nln_prev=v_ln.n):
                for sem, val in fw:
                    sy.wait_ge(sem, val)
                if not fw and nln_prev >= 2:
                    sy.wait_ge(v_ln.h, nln_prev - 1)
                for c in (arms(sy) if callable(row_fn) else [0]):
                    row = row_fn(c) if callable(row_fn) else row_fn
                    sy.dma_start(out=xbuf[:, par * D:(par + 1) * D],
                                 in_=(x3d if callable(row_fn) else x1d)[row:row + 128, :]
                                 ).then_inc(dsem.h, 16)
            op("sync", fs)

            nst = v_st.bump()

            def fv1(v, par=par, nld=nld, dsem=dsem):
                v.wait_ge(dsem.h, nld)
                xb = xbuf[:, par * D:(par + 1) * D]
                st = stats[:, par * 12:par * 12 + 12]
                v.bn_stats(out=st[:, 0:6], in_=xb[:, 0:512])
                v.bn_stats(out=st[:, 6:12], in_=xb[:, 512:1024])
                v.bn_aggr(out=mv[:, par * 2:par * 2 + 2], in_=st)
                v.tensor_scalar(out=veps[:, par:par + 1],
                                in0=mv[:, par * 2 + 1:par * 2 + 2],
                                scalar1=1e-5, scalar2=None, op0=ALU.add)
                v.mark(v_st.h)
            op("vector", fv1)

            nrs = s_rs.bump()

            def fsc(s, par=par, nst=nst, nln_prev=v_ln.n):
                s.wait_ge(v_st.h, nst)
                if nln_prev >= 1:
                    s.wait_ge(v_ln.h, nln_prev)
                s.activation(out=rstd[:, par:par + 1], in_=veps[:, par:par + 1], func=AF.Ln)
                s.activation(out=rstd[:, par:par + 1], in_=rstd[:, par:par + 1],
                             func=AF.Exp, scale=-0.5)
                s.mark(s_rs.h)
            op("scalar", fsc)

            tr0 = t_tr.n
            nln = v_ln.bump()

            def fv2(v, par=par, nrs=nrs, tr0=tr0):
                v.wait_ge(s_rs.h, nrs)
                if tr0 > 0:
                    v.wait_ge(t_tr.h, tr0)
                xb = xbuf[:, par * D:(par + 1) * D]
                v.tensor_scalar(out=lnt[:], in0=xb,
                                scalar1=mv[:, par * 2:par * 2 + 1],
                                scalar2=rstd[:, par:par + 1],
                                op0=ALU.subtract, op1=ALU.mult)
                v.mark(v_ln.h)
            op("vector", fv2)

            for ct in range(NCT):
                ps, guard, g = rotT.acquire()
                ntr = t_tr.bump()

                def ft(t, ct=ct, ps=ps, guard=guard, nln=nln):
                    if ct == 0:
                        t.wait_ge(v_ln.h, nln)
                    wait_guard(t, guard)
                    t.transpose(ps[:, 0:128], lnt[:, 128 * ct:128 * (ct + 1)],
                                identt[:]).then_inc(t_tr.h, 1)
                op("tensor", ft)

                nev = v_ev.bump()

                def fv3(v, ct=ct, ps=ps, ntr=ntr, dst_col=cm_dst_col):
                    v.wait_ge(t_tr.h, ntr)
                    v.tensor_copy(out=cmA[:, 2048 * ct + dst_col: 2048 * ct + dst_col + 128],
                                  in_=ps[:, 0:128])
                    v.mark(v_ev.h)
                op("vector", fv3)
                rotT.release(g, v_ev, nev)
        emit_ln_tile.par = 0

        def emit_ffn_segment(kind, seg, vit_fin, ng8, extra_w_guards=()):
            for ti in range(16):
                fw = []
                if ti == 0 and seg in (0, 4):
                    fw.append((v_it.h, vit_fin))
                if ti < 2:
                    for sm in (v_x2, v_x3):
                        if sm.n > 0:
                            fw.append((sm.h, sm.n))
                if kind == "p3":
                    emit_ln_tile(2048 * seg + 128 * ti, 128 * ti, fw)
                else:
                    emit_ln_tile(lambda c, ti=ti: 128 * (core_T0(c) + ti),
                                 128 * ti, fw)
            LNEV = v_ev.n

            for ui, unit in enumerate(UNITS):
                kindu = unit[0]
                b = ui % 2
                nw1 = d_w1[b].bump(16)
                nw2 = d_w2[b].bump(16)
                uguard = wslot_guard.get(("u", b))

                def fgw_b(g, ui=ui, b=b, uguard=uguard,
                          extra=(tuple(extra_w_guards) if ui < 2 else ())):
                    if uguard is not None:
                        g.wait_ge(uguard[0].h, uguard[1])
                    for sem, val in extra:
                        g.wait_ge(sem, val)
                    g.dma_start(out=fc1sb[:, b * 8192:(b + 1) * 8192].rearrange(
                                    "p (ct m) -> p ct m", m=1024),
                                in_=u1wb[ui].rearrange("(ct p) m -> p ct m", p=128)
                                ).then_inc(d_w1[b].h, 16)
                    g.dma_start(out=fc2sb[:, b * 8192:(b + 1) * 8192].rearrange(
                                    "p (mt c) -> p mt c", c=1024),
                                in_=u2wb[ui].rearrange("(mt p) c -> p mt c", p=128)
                                ).then_inc(d_w2[b].h, 16)
                op("gpsimd", fgw_b)

                for ci in range(4):
                    cst = 512 * ci
                    if kindu == "ex":
                        e = unit[1]
                        psg, guard, gg = rotA.acquire()
                        ngmm = t_mm.bump()

                        def ftgate(t, e=e, psg=psg, guard=guard, cst=cst,
                                   first=(ui == 4 and ci == 0), ng8=ng8):
                            if first:
                                t.wait_ge(v_g8.h, ng8)
                            wait_guard(t, guard)
                            t.matmul(psg[:, 0:512], sel_s[:, 128 * e:128 * (e + 1)],
                                     g8all[:, cst:cst + 512],
                                     start=True, stop=True).then_inc(t_mm.h, 1)
                        op("tensor", ftgate)

                        ngc = v_gc.bump()

                        def fvgate(v, psg=psg, ngmm=ngmm, ci=ci):
                            v.wait_ge(t_mm.h, ngmm)
                            v.tensor_copy(out=gate_sb[:, (ci % 2) * 512:(ci % 2) * 512 + 512],
                                          in_=psg[:, 0:512])
                            v.mark(v_gc.h)
                        op("vector", fvgate)
                        rotA.release(gg, v_gc, ngc)

                    ty_chunk_prev = t_y.n
                    for mt in range(NCT):
                        psm, guard, gg = rotA.acquire()
                        nmm = t_mm.bump()

                        def ftf1(t, ui=ui, b=b, mt=mt, psm=psm, guard=guard,
                                 cst=cst, nw1=nw1, nw2=nw2, ci=ci,
                                 first=(ui == 0 and ci == 0), lnev=LNEV):
                            if ci == 0 and mt == 0:
                                t.wait_ge(d_w1[b].h, nw1)
                                t.wait_ge(d_w2[b].h, nw2)
                                if first:
                                    t.wait_ge(v_ev.h, lnev)
                            wait_guard(t, guard)
                            for ct in range(NCT):
                                mi = t.matmul(psm[:, 0:512],
                                              fc1sb[:, b * 8192 + ct * 1024 + 128 * mt:
                                                    b * 8192 + ct * 1024 + 128 * (mt + 1)],
                                              cmA[:, 2048 * ct + cst: 2048 * ct + cst + 512],
                                              start=(ct == 0), stop=(ct == NCT - 1))
                            mi.then_inc(t_mm.h, 1)
                        op("tensor", ftf1)

                        nsh = s_h.bump()
                        if kindu == "sh":
                            hdst = hbuf[:, mt * 512:mt * 512 + 512]
                        else:
                            hdst = hraw[:, (mt % 2) * 512:(mt % 2) * 512 + 512]

                        def fsg(s, psm=psm, nmm=nmm, hdst=hdst, ui=ui, mt=mt,
                                kindu=kindu, prev_vh=v_h.n, typrev=ty_chunk_prev):
                            s.wait_ge(t_mm.h, nmm)
                            if kindu == "sh" and mt == 0 and typrev > 0:
                                s.wait_ge(t_y.h, typrev)
                            if kindu == "ex" and prev_vh >= 2:
                                s.wait_ge(v_h.h, prev_vh - 1)
                            s.activation(out=hdst, in_=psm[:, 0:512], func=AF.Gelu,
                                         bias=ubias_s[:, ui * 8 + mt:ui * 8 + mt + 1],
                                         scale=1.0)
                            s.mark(s_h.h)
                        op("scalar", fsg)
                        rotA.release(gg, s_h, nsh)

                        if kindu == "ex":
                            nvh = v_h.bump()

                            def fvh(v, mt=mt, ci=ci, nsh=nsh, ngc=v_gc.n,
                                    typrev=ty_chunk_prev):
                                v.wait_ge(s_h.h, nsh)
                                v.wait_ge(v_gc.h, ngc)
                                if mt == 0 and typrev > 0:
                                    v.wait_ge(t_y.h, typrev)
                                v.tensor_mul(out=hbuf[:, mt * 512:mt * 512 + 512],
                                             in0=hraw[:, (mt % 2) * 512:(mt % 2) * 512 + 512],
                                             in1=gate_sb[:, (ci % 2) * 512:(ci % 2) * 512 + 512])
                                v.mark(v_h.h)
                            op("vector", fvh)

                    h_sem, h_n = (s_h, s_h.n) if kindu == "sh" else (v_h, v_h.n)
                    for ct in range(NCT):
                        psv, guardY, gy = rotY.acquire()
                        nty = t_y.bump()

                        def ftf2(t, ui=ui, b=b, ct=ct, cst=cst, psv=psv,
                                 guardY=guardY, h_sem=h_sem, h_n=h_n):
                            if ct == 0:
                                t.wait_ge(h_sem.h, h_n)
                            wait_guard(t, guardY)
                            for mt in range(NCT):
                                mi = t.matmul(psv[:, 0:512],
                                              fc2sb[:, b * 8192 + mt * 1024 + 128 * ct:
                                                    b * 8192 + mt * 1024 + 128 * (ct + 1)],
                                              hbuf[:, mt * 512:mt * 512 + 512],
                                              start=(mt == 0), stop=(mt == NCT - 1))
                            mi.then_inc(t_y.h, 1)
                        op("tensor", ftf2)

                        nvy = v_y.bump()

                        def fvy(v, ui=ui, ct=ct, cst=cst, psv=psv, nty=nty):
                            v.wait_ge(t_y.h, nty)
                            dst = y_cm[:, 2048 * ct + cst: 2048 * ct + cst + 512]
                            if ui == 0:
                                v.tensor_copy(out=dst, in_=psv[:, 0:512])
                            else:
                                v.tensor_add(out=dst, in0=dst, in1=psv[:, 0:512])
                            v.mark(v_y.h)
                        op("vector", fvy)
                        rotY.release(gy, v_y, nvy)
                wslot_guard[("u", b)] = (t_y, t_y.n)

            for ci in range(4):
                cst = 512 * ci
                for ct in range(NCT):
                    psv, guardY, gy = rotY.acquire()
                    nty = t_y.bump()

                    def ftfb(t, ct=ct, cst=cst, psv=psv, guardY=guardY):
                        wait_guard(t, guardY)
                        t.matmul(psv[:, 0:512], fc2bT_s[0:8, 128 * ct:128 * (ct + 1)],
                                 g8all[:, cst:cst + 512], start=True, stop=False)
                        t.matmul(psv[:, 0:512], sh2b_s[0:1, 128 * ct:128 * (ct + 1)],
                                 ones512_s[0:1, 0:512], start=False,
                                 stop=True).then_inc(t_y.h, 1)
                    op("tensor", ftfb)

                    nvy = v_y.bump()

                    def fvyb(v, ct=ct, cst=cst, psv=psv, nty=nty):
                        v.wait_ge(t_y.h, nty)
                        dst = y_cm[:, 2048 * ct + cst: 2048 * ct + cst + 512]
                        v.tensor_add(out=dst, in0=dst, in1=psv[:, 0:512])
                        v.mark(v_y.h)
                    op("vector", fvyb)
                    rotY.release(gy, v_y, nvy)
                VY_CHUNK = v_y.n

                for sub in range(4):
                    ti = 4 * ci + sub
                    par = ti % 2
                    dsem = d_x[par]
                    nld = dsem.bump(16)

                    def fsr2(sy, ti=ti, par=par, dsem=dsem, nvx_prev=v_x2.n,
                             kind=kind, seg=seg, first=(ci == 0 and sub == 0),
                             vitf=vit_fin, lnv=v_ln.n):
                        if first:
                            sy.wait_ge(v_ln.h, lnv)
                        if nvx_prev >= 2 * NCT:
                            sy.wait_ge(v_x2.h, nvx_prev - NCT)
                        if kind == "p3":
                            sy.dma_start(out=xbuf[:, par * D:(par + 1) * D],
                                         in_=x1d[2048 * seg + 128 * ti:
                                                 2048 * seg + 128 * (ti + 1), :]
                                         ).then_inc(dsem.h, 16)
                        else:
                            for c in arms(sy):
                                row = 128 * (core_T0(c) + ti)
                                sy.dma_start(out=xbuf[:, par * D:(par + 1) * D],
                                             in_=x3d[row:row + 128, :]).then_inc(dsem.h, 16)
                    op("sync", fsr2)

                    ndo_prev = d_o[par].n
                    for ct in range(NCT):
                        ps, guard, g = rotT.acquire()
                        ntr = t_tr.bump()

                        def ftt(t, ct=ct, ti=ti, ps=ps, guard=guard, nvy=VY_CHUNK):
                            if ct == 0:
                                t.wait_ge(v_y.h, nvy)
                            wait_guard(t, guard)
                            t.transpose(ps[:, 0:128],
                                        y_cm[:, 2048 * ct + 128 * ti: 2048 * ct + 128 * (ti + 1)],
                                        identt[:]).then_inc(t_tr.h, 1)
                        op("tensor", ftt)

                        nvx = v_x2.bump()

                        def fvx2t(v, ct=ct, ps=ps, ntr=ntr, par=par, nld=nld,
                                  dsem=dsem, ndo_prev=ndo_prev):
                            if ct == 0:
                                v.wait_ge(dsem.h, nld)
                                if ndo_prev >= 16:
                                    v.wait_ge(d_o[par].h, ndo_prev)
                            v.wait_ge(t_tr.h, ntr)
                            v.tensor_add(out=x2t[:, par * D + 128 * ct: par * D + 128 * (ct + 1)],
                                         in0=xbuf[:, par * D + 128 * ct: par * D + 128 * (ct + 1)],
                                         in1=ps[:, 0:128])
                            v.mark(v_x2.h)
                        op("vector", fvx2t)
                        rotT.release(g, v_x2, nvx)

                    ndo = d_o[par].bump(16)

                    def fgo2(g, ti=ti, par=par, nvx=v_x2.n, kind=kind, seg=seg):
                        g.wait_ge(v_x2.h, nvx)
                        dst = (x2d[2048 * seg + 128 * ti: 2048 * seg + 128 * (ti + 1), :]
                               if kind == "p3" else y_out[128 * ti:128 * (ti + 1), :])
                        g.dma_start(out=dst, in_=x2t[:, par * D:(par + 1) * D]
                                    ).then_inc(d_o[par].h, 16)
                    op("gpsimd", fgo2)

        def emit_attn_b(bi, ty_p3, tmm_p3, vit_fin0):
            R = 4096 * bi
            if bi == 0:
                nwf = d_wf.bump(48)

                def fwf(sy, ty_p3=ty_p3, tmm_p3=tmm_p3, vitf=vit_fin0):
                    sy.wait_ge(t_y.h, ty_p3)
                    sy.wait_ge(t_mm.h, tmm_p3)
                    sy.wait_ge(v_it.h, vitf)
                    sy.dma_start(out=wfix_in.rearrange("p (ct m) -> p ct m", m=1024),
                                 in_=in_wb.rearrange("(ct p) m -> p ct m", p=128)
                                 ).then_inc(d_wf.h, 16)
                    sy.dma_start(out=wfix_out.rearrange("p (ct m) -> p ct m", m=1024),
                                 in_=out_wb.rearrange("(ct p) m -> p ct m", p=128)
                                 ).then_inc(d_wf.h, 16)
                    sy.dma_start(out=kern_s, in_=kern_cm).then_inc(d_wf.h, 16)
                op("sync", fwf)
                wslot_guard[("w", 0)] = (t_y, ty_p3)
                wslot_guard[("w", 1)] = (t_y, ty_p3)
            NWF = d_wf.n

            for cj in range(8):
                cst = 512 * cj
                for k in range(4):
                    ti5 = 4 * cj + k
                    par = emit_ln_tile.par
                    emit_ln_tile.par ^= 1
                    dsem = d_x[par]
                    nld = dsem.bump(16)

                    def fs5(sy, ti5=ti5, par=par, dsem=dsem, R=R,
                            first=(cj == 0 and k == 0), nln_prev=v_ln.n,
                            nvx2=v_x2.n, nvx3=v_x3.n):
                        if first:
                            if nvx2 > 0:
                                sy.wait_ge(v_x2.h, nvx2)
                            if nvx3 > 0:
                                sy.wait_ge(v_x3.h, nvx3)
                        if nln_prev >= 2:
                            sy.wait_ge(v_ln.h, nln_prev - 1)
                        sy.dma_start(out=xbuf[:, par * D:(par + 1) * D],
                                     in_=x2d[R + 128 * ti5:R + 128 * (ti5 + 1), :]
                                     ).then_inc(dsem.h, 16)
                    op("sync", fs5)

                    nst = v_st.bump()

                    def fv51(v, par=par, nld=nld, dsem=dsem):
                        v.wait_ge(dsem.h, nld)
                        xb = xbuf[:, par * D:(par + 1) * D]
                        st = stats[:, par * 12:par * 12 + 12]
                        v.bn_stats(out=st[:, 0:6], in_=xb[:, 0:512])
                        v.bn_stats(out=st[:, 6:12], in_=xb[:, 512:1024])
                        v.bn_aggr(out=mv[:, par * 2:par * 2 + 2], in_=st)
                        v.tensor_scalar(out=veps[:, par:par + 1],
                                        in0=mv[:, par * 2 + 1:par * 2 + 2],
                                        scalar1=1e-5, scalar2=None, op0=ALU.add)
                        v.mark(v_st.h)
                    op("vector", fv51)

                    nrs = s_rs.bump()

                    def fs5c(sc, par=par, nst=nst, nln_prev=v_ln.n):
                        sc.wait_ge(v_st.h, nst)
                        if nln_prev >= 1:
                            sc.wait_ge(v_ln.h, nln_prev)
                        sc.activation(out=rstd[:, par:par + 1], in_=veps[:, par:par + 1],
                                      func=AF.Ln)
                        sc.activation(out=rstd[:, par:par + 1], in_=rstd[:, par:par + 1],
                                      func=AF.Exp, scale=-0.5)
                        sc.mark(s_rs.h)
                    op("scalar", fs5c)

                    tr0 = t_tr.n
                    nln = v_ln.bump()

                    def fv52(v, par=par, nrs=nrs, tr0=tr0):
                        v.wait_ge(s_rs.h, nrs)
                        if tr0 > 0:
                            v.wait_ge(t_tr.h, tr0)
                        xb = xbuf[:, par * D:(par + 1) * D]
                        v.tensor_scalar(out=lnt[:], in0=xb,
                                        scalar1=mv[:, par * 2:par * 2 + 1],
                                        scalar2=rstd[:, par:par + 1],
                                        op0=ALU.subtract, op1=ALU.mult)
                        v.mark(v_ln.h)
                    op("vector", fv52)

                    for ct in range(NCT):
                        ps, guard, g = rotT.acquire()
                        ntr = t_tr.bump()

                        def ft5(t, ct=ct, ps=ps, guard=guard, nln=nln):
                            if ct == 0:
                                t.wait_ge(v_ln.h, nln)
                            wait_guard(t, guard)
                            t.transpose(ps[:, 0:128], lnt[:, 128 * ct:128 * (ct + 1)],
                                        identt[:]).then_inc(t_tr.h, 1)
                        op("tensor", ft5)

                        nev = v_ev.bump()

                        def fv53(v, k=k, ct=ct, ps=ps, ntr=ntr):
                            v.wait_ge(t_tr.h, ntr)
                            v.tensor_copy(out=lnchunk[:, 512 * ct + 128 * k:
                                                      512 * ct + 128 * (k + 1)],
                                          in_=ps[:, 0:128])
                            v.mark(v_ev.h)
                        op("vector", fv53)
                        rotT.release(g, v_ev, nev)
                EV5 = v_ev.n

                for cto in range(NCT):
                    psA, guard, g = rotA.acquire()
                    nmm = t_mm.bump()

                    def ftxp(t, cto=cto, psA=psA, guard=guard, nwf=NWF, ev5=EV5,
                             first=(cto == 0)):
                        if first:
                            t.wait_ge(d_wf.h, nwf)
                            t.wait_ge(v_ev.h, ev5)
                        wait_guard(t, guard)
                        for cti in range(NCT):
                            mi = t.matmul(psA[:, 0:512],
                                          wfix_in[:, cti * 1024 + 128 * cto:
                                                  cti * 1024 + 128 * (cto + 1)],
                                          lnchunk[:, 512 * cti:512 * cti + 512],
                                          start=(cti == 0), stop=(cti == NCT - 1))
                        mi.then_inc(t_mm.h, 1)
                    op("tensor", ftxp)

                    nps = v_ps.bump()

                    def fvxp(v, cto=cto, psA=psA, nmm=nmm,
                             first=(cj == 0 and cto == 0),
                             do0=d_o[0].n, do1=d_o[1].n):
                        v.wait_ge(t_mm.h, nmm)
                        if first:
                            if do0 >= 16:
                                v.wait_ge(d_o[0].h, do0)
                            if do1 >= 16:
                                v.wait_ge(d_o[1].h, do1)
                        v.tensor_scalar(out=xproj[:, 512 * cto:512 * cto + 512],
                                        in0=psA[:, 0:512], scalar1=inb_s[:, cto:cto + 1],
                                        scalar2=None, op0=ALU.add)
                        v.mark(v_ps.h)
                    op("vector", fvxp)
                    rotA.release(g, v_ps, nps)

                for ht in range(32):
                    bwm = ht % 2
                    nwm = d_wm[bwm].bump(16)
                    wg = wslot_guard[("w", bwm)]

                    def fgw1(g, ht=ht, bwm=bwm, wg=wg):
                        g.wait_ge(wg[0].h, wg[1])
                        g.dma_start(out=wstream[:, bwm * 4096:bwm * 4096 + 1024].rearrange(
                                        "p (ct m) -> p ct m", m=128),
                                    in_=mod_w1b[:, 128 * ht:128 * (ht + 1)].rearrange(
                                        "(ct p) m -> p ct m", p=128)).then_inc(d_wm[bwm].h, 16)
                    op("sync", fgw1)

                    psA, guard, g = rotA.acquire()
                    nmm = t_mm.bump()
                    wslot_guard[("w", bwm)] = (t_mm, nmm)

                    def ftm1(t, ht=ht, bwm=bwm, psA=psA, guard=guard, nwm=nwm):
                        t.wait_ge(d_wm[bwm].h, nwm)
                        wait_guard(t, guard)
                        for cti in range(NCT):
                            mi = t.matmul(psA[:, 0:512],
                                          wstream[:, bwm * 4096 + 128 * cti:
                                                  bwm * 4096 + 128 * (cti + 1)],
                                          lnchunk[:, 512 * cti:512 * cti + 512],
                                          start=(cti == 0), stop=(cti == NCT - 1))
                        mi.then_inc(t_mm.h, 1)
                    op("tensor", ftm1)

                    nsh = s_h.bump()

                    def fsm1(sc, ht=ht, psA=psA, nmm=nmm, prev_hm=hm_guard[0]):
                        sc.wait_ge(t_mm.h, nmm)
                        if prev_hm is not None:
                            sc.wait_ge(t_mm.h, prev_hm)
                        sc.activation(out=hm[:, 512 * ht:512 * ht + 512],
                                      in_=psA[:, 0:512], func=AF.Gelu,
                                      bias=modb1_s[:, ht:ht + 1], scale=1.0)
                        sc.mark(s_h.h)
                    op("scalar", fsm1)
                    rotA.release(g, s_h, nsh)
                SH5 = s_h.n

                for cto in range(NCT):
                    bwm = cto % 2
                    nwm = d_wm[bwm].bump(16)
                    wg = wslot_guard[("w", bwm)]

                    def fgw2(g, cto=cto, bwm=bwm, wg=wg):
                        g.wait_ge(wg[0].h, wg[1])
                        g.dma_start(out=wstream[:, bwm * 4096:(bwm + 1) * 4096].rearrange(
                                        "p (ht m) -> p ht m", m=128),
                                    in_=mod_w2b[:, 128 * cto:128 * (cto + 1)].rearrange(
                                        "(ht p) m -> p ht m", p=128)).then_inc(d_wm[bwm].h, 16)
                    op("sync", fgw2)

                    psA, guard, g = rotA.acquire()
                    nmm = t_mm.bump()
                    wslot_guard[("w", bwm)] = (t_mm, nmm)

                    def ftm2(t, cto=cto, bwm=bwm, psA=psA, guard=guard, nwm=nwm,
                             sh5=SH5):
                        t.wait_ge(d_wm[bwm].h, nwm)
                        if cto == 0:
                            t.wait_ge(s_h.h, sh5)
                        wait_guard(t, guard)
                        for ht in range(32):
                            mi = t.matmul(psA[:, 0:512],
                                          wstream[:, bwm * 4096 + 128 * ht:
                                                  bwm * 4096 + 128 * (ht + 1)],
                                          hm[:, 512 * ht:512 * ht + 512],
                                          start=(ht == 0), stop=(ht == 31))
                        mi.then_inc(t_mm.h, 1)
                    op("tensor", ftm2)

                    nmo = v_mo.bump()

                    def fvm2(v, cto=cto, psA=psA, nmm=nmm, cst=cst):
                        v.wait_ge(t_mm.h, nmm)
                        v.tensor_scalar(out=sall[:, 0:512], in0=psA[:, 0:512],
                                        scalar1=modb2_s[:, cto:cto + 1], scalar2=None,
                                        op0=ALU.add)
                        v.tensor_mul(out=modu[:, 4160 * cto + 64 + cst:
                                             4160 * cto + 64 + cst + 512],
                                     in0=sall[:, 0:512],
                                     in1=xproj[:, 512 * cto:512 * cto + 512])
                        v.mark(v_mo.h)
                    op("vector", fvm2)
                    rotA.release(g, v_mo, nmo)
                hm_guard[0] = t_mm.n

            nmo_pad = v_mo.bump()

            def fvpad(v):
                for cto in range(NCT):
                    v.tensor_copy(out=modu[:, 4160 * cto:4160 * cto + 64],
                                  in_=modu[:, 4160 * cto + 4096:4160 * cto + 4160])
                v.mark(v_mo.h)
            op("vector", fvpad)
            NMO = v_mo.n
            TMM_MOD = t_mm.n

            for h in range(2):
                for ct in range(NCT):
                    nd = v_diag.bump()

                    def fvd(v, ct=ct, first=(h == 0 and ct == 0), tmm_mod=TMM_MOD,
                            tcv_prev=t_cv.n, nwf=NWF):
                        if first:
                            v.wait_ge(t_mm.h, tmm_mod)
                            v.wait_ge(d_wf.h, nwf)
                        if tcv_prev > 0:
                            v.wait_ge(t_cv.h, tcv_prev)
                        for j in range(64):
                            v.tensor_scalar(out=diag64[:, 128 * j:128 * (j + 1)],
                                            in0=identt[:],
                                            scalar1=kern_s[:, 64 * ct + j:64 * ct + j + 1],
                                            scalar2=None, op0=ALU.mult)
                        v.mark(v_diag.h)
                    op("vector", fvd)

                    for c4r in range(4):
                        c4 = 4 * h + c4r
                        psC, guardC, gC = rotY.acquire()
                        ncv = t_cv.bump()

                        def ftc(t, ct=ct, c4=c4, psC=psC, guardC=guardC, nd=nd,
                                nmo=NMO):
                            t.wait_ge(v_diag.h, nd)
                            if c4 == 0 or (c4 == 4 and ct == 0):
                                t.wait_ge(v_mo.h, nmo)
                            wait_guard(t, guardC)
                            base = 4160 * ct + 64 + 512 * c4
                            for j in range(64):
                                mi = t.matmul(psC[:, 0:512],
                                              diag64[:, 128 * j:128 * (j + 1)],
                                              modu[:, base - j: base - j + 512],
                                              start=(j == 0), stop=(j == 63))
                            mi.then_inc(t_cv.h, 1)
                        op("tensor", ftc)

                        ncs = v_cs.bump()

                        def fvc(v, ct=ct, c4r=c4r, psC=psC, ncv=ncv):
                            v.wait_ge(t_cv.h, ncv)
                            v.tensor_copy(out=conv_full[:, 2048 * ct + 512 * c4r:
                                                        2048 * ct + 512 * (c4r + 1)],
                                          in_=psC[:, 0:512])
                            v.mark(v_cs.h)
                        op("vector", fvc)
                        rotY.release(gC, v_cs, ncs)
                VCS_H0 = v_cs.n - 32

                for c4r in range(4):
                    for cto in range(NCT):
                        psA, guard, g = rotA.acquire()
                        nmm = t_mm.bump()

                        def fto(t, c4r=c4r, cto=cto, psA=psA, guard=guard,
                                needcs=(VCS_H0 + 7 * 4 + c4r + 1)):
                            t.wait_ge(v_cs.h, needcs)
                            wait_guard(t, guard)
                            for cti in range(NCT):
                                mi = t.matmul(psA[:, 0:512],
                                              wfix_out[:, cti * 1024 + 128 * cto:
                                                       cti * 1024 + 128 * (cto + 1)],
                                              conv_full[:, 2048 * cti + 512 * c4r:
                                                        2048 * cti + 512 * (c4r + 1)],
                                              start=(cti == 0), stop=(cti == NCT - 1))
                            mi.then_inc(t_mm.h, 1)
                        op("tensor", fto)

                        nps = v_ps.bump()

                        def fvo(v, cto=cto, psA=psA, nmm=nmm):
                            v.wait_ge(t_mm.h, nmm)
                            v.tensor_scalar(out=osb[:, 512 * cto:512 * (cto + 1)],
                                            in0=psA[:, 0:512],
                                            scalar1=outb_s[:, cto:cto + 1],
                                            scalar2=None, op0=ALU.add)
                            v.mark(v_ps.h)
                        op("vector", fvo)
                        rotA.release(g, v_ps, nps)
                    OSB_PS = v_ps.n

                    for st4 in range(4):
                        til = 16 * h + 4 * c4r + st4
                        par = emit_ln_tile.par
                        emit_ln_tile.par ^= 1
                        dsem = d_x[par]
                        nld = dsem.bump(16)

                        def fsx3(sy, til=til, par=par, dsem=dsem, R=R,
                                 nln_prev=v_ln.n, nx3_prev=v_x3.n):
                            sy.wait_ge(v_ln.h, nln_prev)
                            if nx3_prev >= 2:
                                sy.wait_ge(v_x3.h, nx3_prev - 1)
                            sy.dma_start(out=xbuf[:, par * D:(par + 1) * D],
                                         in_=x2d[R + 128 * til:R + 128 * (til + 1), :]
                                         ).then_inc(dsem.h, 16)
                        op("sync", fsx3)

                        for ct in range(NCT):
                            ps, guard, g = rotT.acquire()
                            ntr = t_tr.bump()

                            def ftt3(t, ct=ct, st4=st4, ps=ps, guard=guard,
                                     ops_n=OSB_PS):
                                if ct == 0:
                                    t.wait_ge(v_ps.h, ops_n)
                                wait_guard(t, guard)
                                t.transpose(ps[:, 0:128],
                                            osb[:, 512 * ct + 128 * st4:
                                                512 * ct + 128 * (st4 + 1)],
                                            identt[:]).then_inc(t_tr.h, 1)
                            op("tensor", ftt3)

                            nx3 = v_x3.bump()

                            def fvx3(v, ct=ct, ps=ps, ntr=ntr, par=par, nld=nld,
                                     dsem=dsem, ndo_prev=d_o[par].n):
                                if ct == 0:
                                    v.wait_ge(dsem.h, nld)
                                    if ndo_prev >= 16:
                                        v.wait_ge(d_o[par].h, ndo_prev)
                                v.wait_ge(t_tr.h, ntr)
                                v.tensor_add(out=x3t[:, par * D + 128 * ct:
                                                     par * D + 128 * (ct + 1)],
                                             in0=xbuf[:, par * D + 128 * ct:
                                                      par * D + 128 * (ct + 1)],
                                             in1=ps[:, 0:128])
                                v.mark(v_x3.h)
                            op("vector", fvx3)
                            rotT.release(g, v_x3, nx3)
                        NX3 = v_x3.n

                        ndo = d_o[par].bump(16)

                        def fgx3(g, til=til, par=par, nx3=NX3, R=R):
                            g.wait_ge(v_x3.h, nx3)
                            g.dma_start(out=x3d[R + 128 * til:R + 128 * (til + 1), :],
                                        in_=x3t[:, par * D:(par + 1) * D]
                                        ).then_inc(d_o[par].h, 16)
                        op("gpsimd", fgx3)

                        nst = v_st.bump()

                        def fvl3(v, par=par):
                            xb = x3t[:, par * D:(par + 1) * D]
                            st = stats[:, par * 12:par * 12 + 12]
                            v.bn_stats(out=st[:, 0:6], in_=xb[:, 0:512])
                            v.bn_stats(out=st[:, 6:12], in_=xb[:, 512:1024])
                            v.bn_aggr(out=mv[:, par * 2:par * 2 + 2], in_=st)
                            v.tensor_scalar(out=veps[:, par:par + 1],
                                            in0=mv[:, par * 2 + 1:par * 2 + 2],
                                            scalar1=1e-5, scalar2=None, op0=ALU.add)
                            v.mark(v_st.h)
                        op("vector", fvl3)

                        nrs = s_rs.bump()

                        def fsl3(sc, par=par, nst=nst, nln_prev=v_ln.n):
                            sc.wait_ge(v_st.h, nst)
                            if nln_prev >= 1:
                                sc.wait_ge(v_ln.h, nln_prev)
                            sc.activation(out=rstd[:, par:par + 1],
                                          in_=veps[:, par:par + 1], func=AF.Ln)
                            sc.activation(out=rstd[:, par:par + 1],
                                          in_=rstd[:, par:par + 1],
                                          func=AF.Exp, scale=-0.5)
                            sc.mark(s_rs.h)
                        op("scalar", fsl3)

                        tr0 = t_tr.n
                        nln = v_ln.bump()

                        def fvn3(v, par=par, nrs=nrs, tr0=tr0):
                            v.wait_ge(s_rs.h, nrs)
                            v.wait_ge(t_tr.h, tr0)
                            xb = x3t[:, par * D:(par + 1) * D]
                            v.tensor_scalar(out=lnt[:], in0=xb,
                                            scalar1=mv[:, par * 2:par * 2 + 1],
                                            scalar2=rstd[:, par:par + 1],
                                            op0=ALU.subtract, op1=ALU.mult)
                            v.mark(v_ln.h)
                        op("vector", fvn3)

                        for ct in range(NCT):
                            ps, guard, g = rotT.acquire()
                            ntr = t_tr.bump()

                            def ftz5(t, ct=ct, ps=ps, guard=guard, nln=nln):
                                if ct == 0:
                                    t.wait_ge(v_ln.h, nln)
                                wait_guard(t, guard)
                                t.transpose(ps[:, 0:128], lnt[:, 128 * ct:128 * (ct + 1)],
                                            identt[:]).then_inc(t_tr.h, 1)
                            op("tensor", ftz5)

                            nev = v_ev.bump()

                            def fvze5(v, ct=ct, ps=ps, ntr=ntr):
                                v.wait_ge(t_tr.h, ntr)
                                v.tensor_copy(out=lnzT[:, 128 * ct:128 * (ct + 1)],
                                              in_=ps[:, 0:128])
                                v.mark(v_ev.h)
                            op("vector", fvze5)
                            rotT.release(g, v_ev, nev)
                        EVZ5 = v_ev.n

                        ntz = t_z.bump()

                        def ftzm5(t, evz=EVZ5, nz_prev=v_z.n):
                            t.wait_ge(v_ev.h, evz)
                            if nz_prev >= 1:
                                t.wait_ge(v_z.h, nz_prev)
                            for ct in range(NCT):
                                mi = t.matmul(psZ[:, 0:8], lnzT[:, 128 * ct:128 * (ct + 1)],
                                              projT_s[:, 8 * ct:8 * (ct + 1)],
                                              start=(ct == 0), stop=(ct == NCT - 1))
                            mi.then_inc(t_z.h, 1)
                        op("tensor", ftzm5)

                        nvz = v_z.bump()

                        def fvzz5(v, zc=32 * bi + til, ntz=ntz,
                                  first=(bi == 0 and h == 0 and c4r == 0 and st4 == 0),
                                  xr0=XR0, dsh0=32 if exchange == "shared" else 0):
                            v.wait_ge(t_z.h, ntz)
                            if first:
                                v.wait_ge(d_z.h, xr0)
                                if dsh0:
                                    v.wait_ge(d_sh.h, dsh0)
                            v.tensor_copy(out=z_sb[:, 8 * zc:8 * (zc + 1)],
                                          in_=psZ[:, 0:8])
                            v.mark(v_z.h)
                        op("vector", fvzz5)
            wslot_guard["p4_mm"] = t_mm.n
            wslot_guard["p4_cv"] = t_cv.n

        phase("p1")
        TS_TOT = t_s.n
        XR0 = 0
        if dbg_rank == 1:
            def fdbg1(g, nvz=NVZ_R0, do0=d_o[0].n, do1=d_o[1].n):
                g.wait_ge(v_z.h, nvz)
                g.dma_start(out=zdbg, in_=z_sb[:]).then_inc(d_z.h, 16)
                g.wait_ge(d_z.h, 16)
                g.wait_ge(d_o[0].h, do0)
                g.wait_ge(d_o[1].h, do1)
            op("gpsimd", fdbg1)
        else:
            XR0 = emit_exchange(0, NVZ_R0)
            ngd0 = emit_router(XR0)
            VIT_FIN0 = emit_bisect(ngd0)
            phase("rt0")
            for seg in range(4):
                ng8 = emit_gates(16, 16 * seg)
                emit_ffn_segment("p3", seg, VIT_FIN0, ng8,
                                 extra_w_guards=([(t_s.h, TS_TOT)] if seg == 0 else ()))
            phase("p3")
            if dbg_rank == 3:
                def fdbg3(g, do0=d_o[0].n, do1=d_o[1].n):
                    g.wait_ge(d_o[0].h, do0)
                    g.wait_ge(d_o[1].h, do1)
                op("gpsimd", fdbg3)
            else:
                TY_P3 = t_y.n
                TMM_P3 = t_mm.n
                for bi in range(2):
                    emit_attn_b(bi, TY_P3, TMM_P3, VIT_FIN0)
                phase("p4")
                NVZ_R1 = v_z.n
                if dbg_rank == 4:
                    def fdbg4(g, nvz=NVZ_R1, do0=d_o[0].n, do1=d_o[1].n):
                        g.wait_ge(v_z.h, nvz)
                        g.dma_start(out=zdbg, in_=z_sb[:]).then_inc(d_z.h, XR0 and 16 or 16)
                        g.wait_ge(d_z.h, XR0 + 16)
                        g.wait_ge(d_o[0].h, do0)
                        g.wait_ge(d_o[1].h, do1)
                    op("gpsimd", fdbg4)
                else:
                    XR1 = emit_exchange(1, NVZ_R1)
                    ngd1 = emit_router(XR1)
                    VIT_FIN1 = emit_bisect(ngd1)
                    phase("rt1")
                    NG8_1 = emit_gates(16)
                    emit_ffn_segment("p6", 4, VIT_FIN1, NG8_1,
                                     extra_w_guards=[(t_mm.h, wslot_guard["p4_mm"]),
                                                     (t_cv.h, wslot_guard["p4_cv"])])

                    def ffin(g, do0=d_o[0].n, do1=d_o[1].n):
                        g.wait_ge(d_o[0].h, do0)
                        g.wait_ge(d_o[1].h, do1)
                    op("gpsimd", ffin)

        phase("end")

        def replay(ename, e):
            lst = OPS[ename]
            prev = 0
            for name, marks in PHB:
                hi = marks[ename] if name != "end" else len(lst)
                if hi > prev:
                    with nc.named_scope(name):
                        for f in lst[prev:hi]:
                            f(e)
                prev = hi
            for f in lst[prev:]:
                f(e)

        with nc.Block() as block:
            @block.sync
            def _(e):
                replay("sync", e)

            @block.gpsimd
            def _(e):
                replay("gpsimd", e)

            @block.tensor
            def _(e):
                replay("tensor", e)

            @block.vector
            def _(e):
                replay("vector", CE(e, chV))

            @block.scalar
            def _(e):
                replay("scalar", CE(e, chS))

    return nc


def prep_fused_inputs(x, q_scale, k_scale, decay_logit, out_scale, proj_w,
                      proj_b, expert_bias, fc1_w, fc1_b, fc2_w, fc2_b,
                      sh1_w, sh1_b, sh2_w, sh2_b, attnb, with_nonce=True):
    f32 = np.float32
    qs_cm = np.ascontiguousarray(np.asarray(q_scale, f32).reshape(NCT, 128).T)
    ks_cm = np.ascontiguousarray(np.asarray(k_scale, f32).reshape(NCT, 128).T)
    pT = np.asarray(proj_w, f32).T
    projT_h = np.ascontiguousarray(pT.reshape(NCT, 128, 8).transpose(1, 0, 2)
                                   .reshape(128, NCT * 8))
    proj_b = np.asarray(proj_b, f32).reshape(8)
    expert_bias = np.asarray(expert_bias, f32).reshape(8)
    pb8 = np.broadcast_to(proj_b[None, :], (128, 8)).copy()
    eb8 = np.broadcast_to(expert_bias[None, :], (128, 8)).copy()
    u1 = np.zeros((NU, D, D), f32)
    u2 = np.zeros((NU, D, D), f32)
    ub = np.zeros((128, NU * 8), f32)
    for ui, unit in enumerate(UNITS):
        if unit[0] == "sh":
            q = unit[1]
            u1[ui] = sh1_w[1024 * q:1024 * (q + 1), :].T
            u2[ui] = sh2_w[:, 1024 * q:1024 * (q + 1)].T
            bb = sh1_b[1024 * q:1024 * (q + 1)]
        else:
            e, h = unit[1], unit[2]
            u1[ui] = fc1_w[e, 1024 * h:1024 * (h + 1), :].T
            u2[ui] = fc2_w[e][:, 1024 * h:1024 * (h + 1)].T
            bb = fc1_b[e, 1024 * h:1024 * (h + 1)]
        ub[:, ui * 8:(ui + 1) * 8] = bb.reshape(8, 128).T

    def cm(v, nct):
        return np.ascontiguousarray(np.asarray(v, f32).reshape(nct, 128).T)

    a = attnb
    kern = np.asarray(a["kernel"], f32)
    base = {
        "qs_cm": qs_cm, "ks_cm": ks_cm,
        "scal": np.array([[float(decay_logit), float(out_scale), 0.0, 0.0]], f32),
        "projT": projT_h, "pb8": pb8, "eb8": eb8,
        "u1wb": u1.astype(bfd), "u2wb": u2.astype(bfd), "ubias": ub,
        "fc2bT": np.asarray(fc2_b, f32), "sh2b_row": np.asarray(sh2_b, f32).reshape(1, D),
        "in_wb": np.asarray(a["in_w"], f32).astype(bfd), "in_b_cm": cm(a["in_b"], 8),
        "mod_w1b": np.asarray(a["mod_w1"], f32).astype(bfd), "mod_b1_cm": cm(a["mod_b1"], 32),
        "mod_w2b": np.asarray(a["mod_w2"], f32).astype(bfd), "mod_b2_cm": cm(a["mod_b2"], 8),
        "kern_cm": np.ascontiguousarray(
            kern.reshape(8, 128, 64).transpose(1, 0, 2).reshape(128, 512)),
        "out_wb": np.asarray(a["out_w"], f32).astype(bfd), "out_b_cm": cm(a["out_b"], 8),
    }
    if with_nonce:
        rng = np.random.default_rng()
        base["nonce"] = np.array([[rng.integers(1, 2**31), 0, 0, 0]], np.uint32)
    maps = []
    for c in range(8):
        b0 = 0 if c % 2 == 0 else 2
        m = dict(base)
        m["xin"] = np.ascontiguousarray(
            np.asarray(x, np.float32)[b0:b0 + 2].reshape(8192, D))
        maps.append(m)
    return maps


_NC_CACHE = {}


def _get_nc():
    if "f" not in _NC_CACHE:
        _NC_CACHE["f"] = build_fused(cora=None, exchange="shared", rc=False, dbg=None)
    return _NC_CACHE["f"]


def kernel(**inputs):
    from concourse.bass_utils import run_bass_kernel_spmd
    f32 = np.float32
    g = lambda k: np.asarray(inputs[k], f32)
    x = g("L0_attn_a_x")
    dlog = float(np.asarray(inputs["L0_attn_a_decay_logit"]))
    osc = float(np.asarray(inputs["L0_attn_a_out_scale"]))
    attnb = {
        "in_w": g("L0_attn_b_in_w"), "in_b": g("L0_attn_b_in_b"),
        "mod_w1": g("L0_attn_b_mod_w1"), "mod_b1": g("L0_attn_b_mod_b1"),
        "mod_w2": g("L0_attn_b_mod_w2"), "mod_b2": g("L0_attn_b_mod_b2"),
        "kernel": g("L0_attn_b_kernel"), "out_w": g("L0_attn_b_out_w"),
        "out_b": g("L0_attn_b_out_b"),
    }
    nc = _get_nc()
    maps = prep_fused_inputs(
        x, g("L0_attn_a_q_scale"), g("L0_attn_a_k_scale"), dlog, osc,
        g("L0_ffn_proj_w"), g("L0_ffn_proj_b"), g("L0_ffn_expert_bias"),
        g("L0_ffn_fc1_w"), g("L0_ffn_fc1_b"), g("L0_ffn_fc2_w"), g("L0_ffn_fc2_b"),
        g("L0_ffn_sh1_w"), g("L0_ffn_sh1_b"), g("L0_ffn_sh2_w"), g("L0_ffn_sh2_b"),
        attnb, with_nonce=True)
    res = run_bass_kernel_spmd(nc, maps, core_ids=list(range(8)))
    y = np.zeros((4, 4096, 1024), f32)
    for c in range(8):
        bsel = (2 if c % 2 else 0) + (c // 2) // 2
        hsel = (c // 2) % 2
        y[bsel, 2048 * hsel:2048 * (hsel + 1)] = res.results[c]["y_out"]
    return y


# version 05 (final)
# speedup vs baseline: 2.1318x; 2.1318x over previous
import numpy as np
import ml_dtypes
import concourse.bass as bass
import concourse.mybir as mybir

F32 = mybir.dt.float32
BF16 = mybir.dt.bfloat16
U32 = mybir.dt.uint32
AF = mybir.ActivationFunctionType
ALU = mybir.AluOpType
AX = mybir.AxisListType

D = 1024
NCT = 8
NALL = 16384
NZT = 128
CAP = 5120.0
NITER = 60
bfd = ml_dtypes.bfloat16

UNITS = [("sh", q, None) for q in range(4)] + [("ex", e, h) for e in range(8) for h in range(2)]
NU = len(UNITS)


class CE:

    def __init__(self, eng, sem):
        self.e, self.s, self.n = eng, sem, 0

    def wait_ge(self, sem, v):
        self.e.wait_ge(sem, v)

    def mark(self, sem, val=1):
        inst = self.e.nop()
        if self.n > 0:
            inst._wait_ge(self.s, self.n)
        inst.then_inc(sem, val)
        return inst

    def __getattr__(self, name):
        fn = getattr(self.e, name)

        def call(*a, **k):
            inst = fn(*a, **k)
            if self.n > 0:
                inst._wait_ge(self.s, self.n)
            self.n += 1
            inst.then_inc(self.s, 1)
            return inst

        return call


class Sem:
    def __init__(self, handle):
        self.h = handle
        self.n = 0

    def bump(self, by=1):
        self.n += by
        return self.n


class Rot:

    def __init__(self, tensors):
        self.t = tensors
        self.free = {}
        self.g = 0

    def acquire(self):
        g = self.g
        self.g += 1
        return self.t[g % len(self.t)], self.free.get(g - len(self.t)), g

    def release(self, g, sem, n):
        self.free[g] = (sem, n)


def core_T0(c):
    return 32 * ((c // 2) // 2) + 16 * ((c // 2) % 2)


def build_fused(cora=None, exchange="shared", rc=False, dbg=None):
    nc = bass.Bass("TRN2", target_bir_lowering=False, detect_race_conditions=rc)

    def din(name, shape, dt=F32):
        return nc.dram_tensor(name, shape, dt, kind="ExternalInput").ap()

    xin = din("xin", [2 * 4096, D])
    qs_cm = din("qs_cm", [128, NCT])
    ks_cm = din("ks_cm", [128, NCT])
    scal = din("scal", [1, 4])
    projT = din("projT", [128, NCT * 8])
    pb8 = din("pb8", [128, 8])
    eb8 = din("eb8", [128, 8])
    u1wb = din("u1wb", [NU, D, D], BF16)
    u2wb = din("u2wb", [NU, D, D], BF16)
    ubias = din("ubias", [128, NU * 8])
    fc2bT = din("fc2bT", [8, D])
    sh2b_row = din("sh2b_row", [1, D])
    in_wb = din("in_wb", [D, D], BF16)
    in_b_cm = din("in_b_cm", [128, 8])
    mod_w1b = din("mod_w1b", [D, 4096], BF16)
    mod_b1_cm = din("mod_b1_cm", [128, 32])
    mod_w2b = din("mod_w2b", [4096, D], BF16)
    mod_b2_cm = din("mod_b2_cm", [128, 8])
    kern_cm = din("kern_cm", [128, 8 * 64])
    out_wb = din("out_wb", [D, D], BF16)
    out_b_cm = din("out_b_cm", [128, 8])

    dbg_rank = {"p1": 1, "p3": 3, "p4": 4, None: 9}[dbg]
    okind = "ExternalOutput"
    x1d = nc.dram_tensor("x1d", [8192, D], F32,
                         kind=(okind if dbg_rank == 1 else "Internal")).ap()
    x2d = nc.dram_tensor("x2d", [8192, D], F32,
                         kind=(okind if dbg_rank == 3 else "Internal")).ap()
    x3d = nc.dram_tensor("x3d", [8192, D], F32,
                         kind=(okind if dbg_rank == 4 else "Internal")).ap()
    if dbg_rank in (1, 4):
        zdbg = nc.dram_tensor("zdbg", [128, 512], F32, kind=okind).ap()
    if dbg_rank == 9:
        y_out = nc.dram_tensor("y_out", [2048, D], F32, kind=okind).ap()
    wdram = nc.dram_tensor("wdram", [128, 256], F32).ap()
    edram = nc.dram_tensor("edram", [128, 256], F32).ap()
    if exchange == "shared":
        nonce = din("nonce", [1, 4], U32)
        shz = nc.dram_tensor("shz", [256, 1024], F32, addr_space="Shared").ap()
        shflag = nc.dram_tensor("shflag", [2, 128], U32, addr_space="Shared").ap()
    else:
        pz0 = din("pz0", [128, 512])
        pz1 = din("pz1", [128, 512])

    ident_np = np.eye(128, dtype=bfd)
    Rm_np = (np.arange(128)[:, None] - np.arange(128)[None, :]).astype(np.float32)
    Dm_np = np.broadcast_to(128.0 * np.arange(32)[None, :] - 1.0, (128, 32)).astype(np.float32).copy()
    identb = nc.inline_tensor(ident_np, "identb").ap()
    RmT = nc.inline_tensor(Rm_np, "Rm").ap()
    DmT = nc.inline_tensor(Dm_np, "Dm").ap()
    onesT = nc.inline_tensor(np.ones((1, 128), np.float32), "ones_row").ap()
    ecol_np = (np.arange(128)[:, None] // 16).astype(np.float32)
    Esel_np = (ecol_np == np.arange(8)[None, :]).astype(np.float32)
    e_colC = nc.inline_tensor(ecol_np, "e_colC").ap()
    EselC = nc.inline_tensor(Esel_np, "EselC").ap()
    EselTC = nc.inline_tensor(np.ascontiguousarray(Esel_np.T), "EselTC").ap()
    ones512C = nc.inline_tensor(np.ones((1, 512), bfd), "ones512C").ap()
    iota8C = nc.inline_tensor(np.broadcast_to(np.arange(8).astype(np.float32)[None, :],
                                              (128, 8)).copy(), "iota8C").ap()
    ident8C = nc.inline_tensor(np.eye(8, dtype=np.float32), "ident8C").ap()
    ones8x128C = nc.inline_tensor(np.ones((8, 128), np.float32), "ones8x128C").ap()
    selC = []
    for e in range(8):
        m = np.zeros((8, 128), bfd)
        m[e, :] = 1.0
        selC.append(nc.inline_tensor(m, f"sel{e}C").ap())

    from contextlib import ExitStack
    with ExitStack() as ctx:
        E = ctx.enter_context
        arV = E(nc.sbuf_tensor("arV", [128, 33280], BF16))
        arK = E(nc.sbuf_tensor("arK", [128, 32768], BF16))
        arW = E(nc.sbuf_tensor("arW", [128, 8192], BF16))
        V = arV
        K = arK
        Wp = arW[:, 0:4096]
        Qt = arW[:, 4096:5120]
        sw = arW[:, 5120:5376]
        cmA = arV[:, 0:16384]
        y_cm = arV[:, 16384:32768]
        fc1sb = arK[:, 0:16384]
        fc2sb = arK[:, 16384:32768]
        hbuf = arW[:, 0:4096]
        modu = arV
        wfix_in = arK[:, 0:8192]
        wfix_out = arK[:, 8192:16384]
        hm = arK[:, 16384:32768]
        conv_full = arK[:, 16384:32768]
        wstream = arW
        diag64 = arW[:, 0:8192]

        xbuf = E(nc.sbuf_tensor("xbuf", [128, 2 * D], F32))
        x1t = E(nc.sbuf_tensor("x1t", [128, 2 * D], F32))
        x2t = E(nc.sbuf_tensor("x2t", [128, 2 * D], F32))
        wge = xbuf
        epe = x1t
        x3t = x2t
        xproj = x2t[:].bitcast(BF16)
        lnchunk = E(nc.sbuf_tensor("lnchunk", [128, NCT * 512], BF16))
        osb = lnchunk[:, 0:4096]
        zsb = x2t[:, 0:1024]
        sall = E(nc.sbuf_tensor("sall", [128, 1024], F32))
        g8all = E(nc.sbuf_tensor("g8all", [8, 2048], BF16))
        z_sb = E(nc.sbuf_tensor("z_sb", [128, 512], F32))
        lnt = E(nc.sbuf_tensor("lnt", [128, D], BF16))
        lnzT = E(nc.sbuf_tensor("lnzT", [128, NCT * 128], BF16))
        x1t_b = x1t[:].bitcast(BF16)
        hraw = x1t_b[:, 0:1024]
        gate_sb = x1t_b[:, 1024:2048]
        wgrid = E(nc.sbuf_tensor("wgrid", [128, 256], F32))
        egrid = E(nc.sbuf_tensor("egrid", [128, 256], F32))
        identt = E(nc.sbuf_tensor("identt", [128, 128], BF16))
        Rl = x2t[:, 1024:1152]
        Dl = x2t[:, 1152:1184]
        Rm_s = x2t[:, 1440:1568]
        Dm_s = x2t[:, 1568:1600]
        ones_s = E(nc.sbuf_tensor("ones_s", [1, 128], F32))
        ones512_s = E(nc.sbuf_tensor("ones512_s", [1, 512], BF16))
        scal_s = E(nc.sbuf_tensor("scal_s", [1, 4], F32))
        scalc = E(nc.sbuf_tensor("scalc", [128, 4], F32))
        lnd = E(nc.sbuf_tensor("lnd", [128, 1], F32))
        sigd = E(nc.sbuf_tensor("sigd", [128, 1], F32))
        qs_s = E(nc.sbuf_tensor("qs_s", [128, NCT], F32))
        ks_s = E(nc.sbuf_tensor("ks_s", [128, NCT], F32))
        projT_s = E(nc.sbuf_tensor("projT_s", [128, NCT * 8], BF16))
        pb_s = E(nc.sbuf_tensor("pb_s", [128, 8], F32))
        eb_s = E(nc.sbuf_tensor("eb_s", [128, 8], F32))
        ubias_s = E(nc.sbuf_tensor("ubias_s", [128, NU * 8], F32))
        fc2bT_s = E(nc.sbuf_tensor("fc2bT_s", [8, D], BF16))
        sh2b_s = E(nc.sbuf_tensor("sh2b_s", [1, D], BF16))
        ecol_s = E(nc.sbuf_tensor("ecol_s", [128, 1], F32))
        esel_s = E(nc.sbuf_tensor("esel_s", [128, 8], F32))
        eselT_s = E(nc.sbuf_tensor("eselT_s", [8, 128], F32))
        iota8_s = E(nc.sbuf_tensor("iota8_s", [128, 8], F32))
        ident8_s = E(nc.sbuf_tensor("ident8_s", [8, 8], F32))
        ones8x128_s = E(nc.sbuf_tensor("ones8x128_s", [8, 128], F32))
        sel_s = E(nc.sbuf_tensor("sel_s", [8, 8 * 128], BF16))
        inb_s = E(nc.sbuf_tensor("inb_s", [128, 8], F32))
        modb1_s = E(nc.sbuf_tensor("modb1_s", [128, 32], F32))
        modb2_s = E(nc.sbuf_tensor("modb2_s", [128, 8], F32))
        outb_s = E(nc.sbuf_tensor("outb_s", [128, 8], F32))
        kern_s = x1t[:, 512:1024]
        stats = E(nc.sbuf_tensor("stats", [128, 2 * 12], F32))
        mv = E(nc.sbuf_tensor("mv", [128, 2 * 2], F32))
        veps = E(nc.sbuf_tensor("veps", [128, 2], F32))
        rstd = E(nc.sbuf_tensor("rstd", [128, 2], F32))
        wtmp = x2t[:, 1184:1312]
        getmp = x2t[:, 1312:1440]
        m8 = E(nc.sbuf_tensor("m8", [128, 8], F32))
        i8 = E(nc.sbuf_tensor("i8", [128, 8], U32))
        rtmp = E(nc.sbuf_tensor("rtmp", [128, 16], F32))
        lohi = E(nc.sbuf_tensor("lohi", [8, 8], F32))
        cnt_col = E(nc.sbuf_tensor("cnt_col", [128, 1], F32))
        lo_bc = E(nc.sbuf_tensor("lo_bc", [128, 8], F32))
        X8 = E(nc.sbuf_tensor("X8", [8, 8], F32))
        gtm = E(nc.sbuf_tensor("gtm", [128, 8], BF16))
        geq = E(nc.sbuf_tensor("geq", [128, 2 * 8], F32))
        if exchange == "shared":
            nt_s = E(nc.sbuf_tensor("nt_s", [1, 4], U32))
            ft_s = E(nc.sbuf_tensor("ft_s", [1, 128], U32))

        ps0 = E(nc.psum_tensor("ps0", [128, 512], F32))
        ps1 = E(nc.psum_tensor("ps1", [128, 512], F32))
        psY0 = E(nc.psum_tensor("psY0", [128, 512], F32))
        psY1 = E(nc.psum_tensor("psY1", [128, 512], F32))
        psT0 = E(nc.psum_tensor("psT0", [128, 1024], BF16))
        psT1 = E(nc.psum_tensor("psT1", [128, 1024], BF16))
        psZ = E(nc.psum_tensor("psZ", [128, 512], F32))
        psB = E(nc.psum_tensor("psB", [128, 512], F32))
        psA0, psA1 = psY0, psY1

        def S(name):
            return Sem(E(nc.semaphore(name)))

        d_pre = S("d_pre")
        d_x = [S("d_x0"), S("d_x1")]
        d_o = [S("d_o0"), S("d_o1")]
        d_z = S("d_z")
        d_sh = S("d_sh")
        d_gr = S("d_gr")
        d_w1 = [S("d_w1a"), S("d_w1b")]
        d_w2 = [S("d_w2a"), S("d_w2b")]
        d_wm = [S("d_wm0"), S("d_wm1")]
        d_wf = S("d_wf")
        t_tr = S("t_tr")
        t_s = S("t_s")
        t_m2 = S("t_m2")
        t_z = S("t_z")
        t_mm = S("t_mm")
        t_y = S("t_y")
        t_cnt = S("t_cnt")
        t_thr = S("t_thr")
        t_x8 = S("t_x8")
        t_cv = S("t_cv")
        v_st = S("v_st")
        s_rs = S("s_rs")
        v_nrm = S("v_nrm")
        v_ev = S("v_ev")
        v_w = S("v_w")
        v_x1 = S("v_x1")
        v_ln = S("v_ln")
        v_z = S("v_z")
        s_w = S("s_w")
        v_pw = S("v_pw")
        v_zb = S("v_zb")
        s_sg = S("s_sg")
        v_gd = S("v_gd")
        v_it = S("v_it")
        v_gt = S("v_gt")
        v_g8 = S("v_g8")
        v_gc = S("v_gc")
        v_h = S("v_h")
        s_h = S("s_h")
        v_y = S("v_y")
        v_x2 = S("v_x2")
        v_diag = S("v_diag")
        v_cs = S("v_cs")
        v_mo = S("v_mo")
        v_ps = S("v_ps")
        v_x3 = S("v_x3")
        chV = E(nc.semaphore("chV"))
        chS = E(nc.semaphore("chS"))

        OPS = {e: [] for e in ("sync", "gpsimd", "tensor", "vector", "scalar")}
        PHB = []

        def phase(name):
            PHB.append((name, {e: len(OPS[e]) for e in OPS}))

        def op(eng, fn):
            OPS[eng].append(fn)

        rotA = Rot([ps0, ps1])
        rotY = Rot([psY0, psY1])
        rotT = Rot([psT0, psT1])

        def wait_guard(e, guard):
            if guard is not None:
                e.wait_ge(guard[0].h, guard[1])

        def arms(g):
            if cora is not None:
                yield cora
            else:
                for arm in nc.switch_core_id(engines=[g], n=128):
                    yield arm.logical % 8

        pre_list = [
            (identt[:], identb), (Rm_s[:], RmT), (Dm_s[:], DmT), (ones_s[:], onesT),
            (scal_s[:], scal), (qs_s[:], qs_cm), (ks_s[:], ks_cm), (projT_s[:], projT),
            (pb_s[:], pb8), (eb_s[:], eb8), (ubias_s[:], ubias), (fc2bT_s[:], fc2bT),
            (sh2b_s[:], sh2b_row), (ecol_s[:], e_colC), (esel_s[:], EselC),
            (eselT_s[:], EselTC), (ones512_s[:], ones512C), (iota8_s[:], iota8C),
            (ident8_s[:], ident8C), (ones8x128_s[:], ones8x128C),
            (inb_s[:], in_b_cm), (modb1_s[:], mod_b1_cm), (modb2_s[:], mod_b2_cm),
            (outb_s[:], out_b_cm),
        ] + [(sel_s[:, 128 * e:128 * (e + 1)], selC[e]) for e in range(8)]
        if exchange == "shared":
            pre_list.append((nt_s[:], nonce[:, 0:4]))

        for dst, src in pre_list:
            d_pre.bump(16)

            def f(g, dst=dst, src=src):
                g.dma_start(out=dst, in_=src).then_inc(d_pre.h, 16)
            op("gpsimd", f)
        NPRE = d_pre.n

        def ftp(t):
            t.wait_ge(d_pre.h, NPRE)
            t.matmul(psB[:, 0:4], ones_s[:], scal_s[:], start=True,
                     stop=True).then_inc(t_s.h, 1)
        op("tensor", ftp)
        t_s.bump()

        def fvp(v):
            v.wait_ge(t_s.h, 1)
            v.tensor_copy(out=scalc[:], in_=psB[:, 0:4])
            v.mark(v_pw.h)
        op("vector", fvp)
        v_pw.bump()

        def fsp(s):
            s.wait_ge(v_pw.h, 1)
            s.activation(out=sigd[:], in_=scalc[:, 0:1], func=AF.Sigmoid)
            s.mark(s_w.h)
        op("scalar", fsp)
        s_w.bump()

        def fvp2(v):
            v.wait_ge(s_w.h, 1)
            v.tensor_scalar(out=getmp[:, 0:1], in0=sigd[:], scalar1=1e-38, scalar2=None,
                            op0=ALU.max)
            v.mark(v_pw.h)
        op("vector", fvp2)
        v_pw.bump()

        def fsp2(s):
            s.wait_ge(v_pw.h, 2)
            s.activation(out=lnd[:], in_=getmp[:, 0:1], func=AF.Ln)
            s.mark(s_w.h)
        op("scalar", fsp2)
        s_w.bump()

        def fvp3(v):
            v.wait_ge(s_w.h, 2)
            v.tensor_scalar(out=Rl[:], in0=Rm_s[:], scalar1=lnd[:], scalar2=None, op0=ALU.mult)
            v.tensor_scalar(out=Dl[:], in0=Dm_s[:], scalar1=lnd[:], scalar2=None, op0=ALU.mult)
        op("vector", fvp3)

        for m in range(32):
            npw = v_pw.bump()

            def fvw(v, m=m, nsw_prev=s_w.n):
                if m >= 1:
                    v.wait_ge(s_w.h, nsw_prev)
                v.tensor_scalar(out=wtmp[:], in0=Rl[:], scalar1=Dl[:, m:m + 1],
                                scalar2=0.0, op0=ALU.add, op1=ALU.min)
                v.mark(v_pw.h)
            op("vector", fvw)
            nsw = s_w.bump()

            def fsw(s, m=m, npw=npw):
                s.wait_ge(v_pw.h, npw)
                s.activation(out=Wp[:, 128 * m:128 * (m + 1)], in_=wtmp[:], func=AF.Exp)
                s.mark(s_w.h)
            op("scalar", fsw)

        npw_mask = v_pw.bump()

        def fvmask(v, nsw=s_w.n):
            v.wait_ge(s_w.h, nsw)
            v.tensor_scalar(out=getmp[:], in0=Rl[:], scalar1=Dl[:, 0:1],
                            scalar2=0.0, op0=ALU.add, op1=ALU.is_le)
            v.tensor_mul(out=Wp[:, 0:128], in0=Wp[:, 0:128], in1=getmp[:])
            v.mark(v_pw.h)
        op("vector", fvmask)
        NPW = v_pw.n

        def emit_attn_half(xrow_of, TS, zcol_of, frame_off=0):
            TQ = 16
            pairs = [(q, s) for q in range(TQ) for s in range(q, TS)]
            first_idx = {}
            last_idx = {}
            for i, (q, s) in enumerate(pairs):
                if s == q:
                    first_idx[q] = i
                if s == TS - 1:
                    last_idx[q] = i
            VX1_PREV = v_x1.n

            for i in (range(TS) if frame_off == 0 else ()):
                par = i % 2
                dsem = d_x[par]
                nld = dsem.bump(16)

                def fs(sy, i=i, par=par, dsem=dsem, nld=nld, nnrm_prev=v_nrm.n,
                       row=xrow_of(i), first=(i == 0), vx1p=VX1_PREV):
                    if first:
                        sy.wait_ge(d_pre.h, NPRE)
                    if i < 2 and vx1p >= 1:
                        sy.wait_ge(v_x1.h, vx1p)
                    if nnrm_prev >= 2:
                        sy.wait_ge(v_nrm.h, nnrm_prev - 1)
                    sy.dma_start(out=xbuf[:, par * D:(par + 1) * D],
                                 in_=xin[row:row + 128, :]).then_inc(dsem.h, 16)
                op("sync", fs)

                nst = v_st.bump()

                def fv1(v, par=par, nld=nld, dsem=dsem):
                    v.wait_ge(dsem.h, nld)
                    xb = xbuf[:, par * D:(par + 1) * D]
                    st = stats[:, par * 12:par * 12 + 12]
                    v.bn_stats(out=st[:, 0:6], in_=xb[:, 0:512])
                    v.bn_stats(out=st[:, 6:12], in_=xb[:, 512:1024])
                    v.bn_aggr(out=mv[:, par * 2:par * 2 + 2], in_=st)
                    v.tensor_scalar(out=veps[:, par:par + 1],
                                    in0=mv[:, par * 2 + 1:par * 2 + 2],
                                    scalar1=1e-5, scalar2=None, op0=ALU.add)
                    v.mark(v_st.h)
                op("vector", fv1)

                nrs = s_rs.bump()

                def fsc(s, par=par, nst=nst, nnrm_prev=v_nrm.n):
                    s.wait_ge(v_st.h, nst)
                    if nnrm_prev >= 1:
                        s.wait_ge(v_nrm.h, nnrm_prev)
                    s.activation(out=rstd[:, par:par + 1], in_=veps[:, par:par + 1], func=AF.Ln)
                    s.activation(out=rstd[:, par:par + 1], in_=rstd[:, par:par + 1],
                                 func=AF.Exp, scale=-0.5)
                    s.mark(s_rs.h)
                op("scalar", fsc)

                nnrm = v_nrm.bump()

                def fv2(v, i=i, par=par, nrs=nrs):
                    v.wait_ge(s_rs.h, nrs)
                    xb = xbuf[:, par * D:(par + 1) * D]
                    v.tensor_scalar(out=V[:, D * i:D * (i + 1)], in0=xb,
                                    scalar1=mv[:, par * 2:par * 2 + 1],
                                    scalar2=rstd[:, par:par + 1],
                                    op0=ALU.subtract, op1=ALU.mult)
                    v.mark(v_nrm.h)
                op("vector", fv2)

                for ct in range(NCT):
                    ps, guard, g = rotT.acquire()
                    ntr = t_tr.bump()

                    def ft(t, i=i, ct=ct, ps=ps, guard=guard, nnrm=nnrm):
                        if ct == 0:
                            t.wait_ge(v_nrm.h, nnrm)
                        wait_guard(t, guard)
                        t.transpose(ps[:, 0:128], V[:, D * i + 128 * ct: D * i + 128 * (ct + 1)],
                                    identt[:]).then_inc(t_tr.h, 1)
                    op("tensor", ft)

                    nev = v_ev.bump()

                    def fv3(v, i=i, ct=ct, ps=ps, ntr=ntr):
                        v.wait_ge(t_tr.h, ntr)
                        v.tensor_scalar(out=K[:, 4096 * ct + 128 * i: 4096 * ct + 128 * (i + 1)],
                                        in0=ps[:, 0:128], scalar1=ks_s[:, ct:ct + 1],
                                        scalar2=None, op0=ALU.mult)
                        v.mark(v_ev.h)
                    op("vector", fv3)
                    rotT.release(g, v_ev, nev)

            for q in range(TQ):
                par = q % 2
                dsem = d_x[par]
                nld0 = dsem.bump(16)

                def fsq(sy, par=par, dsem=dsem, row=xrow_of(q), nx1_prev=v_x1.n,
                        q=q, nrm_all=v_nrm.n):
                    if q < 2:
                        sy.wait_ge(v_nrm.h, nrm_all)
                        if nx1_prev >= 1:
                            sy.wait_ge(v_x1.h, nx1_prev)
                    else:
                        sy.wait_ge(v_x1.h, nx1_prev - 1)
                    sy.dma_start(out=xbuf[:, par * D:(par + 1) * D],
                                 in_=xin[row:row + 128, :]).then_inc(dsem.h, 16)
                op("sync", fsq)

                for ct in range(NCT):
                    ps, guard, g = rotT.acquire()
                    ntr = t_tr.bump()

                    def ftq(t, q=q + frame_off, ct=ct, ps=ps, guard=guard):
                        wait_guard(t, guard)
                        t.transpose(ps[:, 0:128], V[:, D * q + 128 * ct: D * q + 128 * (ct + 1)],
                                    identt[:]).then_inc(t_tr.h, 1)
                    op("tensor", ftq)

                    nev = v_ev.bump()

                    def fvq(v, ct=ct, ps=ps, ntr=ntr, npw=NPW):
                        v.wait_ge(t_tr.h, ntr)
                        if ct == 0:
                            v.wait_ge(v_pw.h, npw)
                        v.tensor_scalar(out=Qt[:, 128 * ct:128 * (ct + 1)],
                                        in0=ps[:, 0:128], scalar1=qs_s[:, ct:ct + 1],
                                        scalar2=None, op0=ALU.mult)
                        v.mark(v_ev.h)
                    op("vector", fvq)
                    rotT.release(g, v_ev, nev)
                QEV = v_ev.n

                mm2_pend = []
                for idx in range(first_idx[q], last_idx[q] + 1):
                    (qq, s) = pairs[idx]
                    fi = (s == q)
                    la = (s == TS - 1)
                    slot = idx % 2
                    nts = t_s.bump()

                    def fts(t, s=s + frame_off, slot=slot, fi=fi, qev=QEV,
                            nvw_prev=v_w.n):
                        if fi:
                            t.wait_ge(v_ev.h, qev)
                        if nvw_prev >= 2:
                            t.wait_ge(v_w.h, nvw_prev - 1)
                        psS = ps0 if slot == 0 else ps1
                        for ct in range(NCT):
                            mi = t.matmul(psS[:, 0:128],
                                          K[:, 4096 * ct + 128 * s: 4096 * ct + 128 * (s + 1)],
                                          Qt[:, 128 * ct:128 * (ct + 1)],
                                          start=(ct == 0), stop=(ct == NCT - 1))
                        mi.then_inc(t_s.h, 1)
                    op("tensor", fts)

                    nvw = v_w.bump()

                    def fvw2(v, q=q, s=s, slot=slot, nts=nts, nm2_prev=t_m2.n):
                        v.wait_ge(t_s.h, nts)
                        if nm2_prev >= 2:
                            v.wait_ge(t_m2.h, nm2_prev - 1)
                        psS = ps0 if slot == 0 else ps1
                        m = s - q
                        v.tensor_mul(out=sw[:, slot * 128:(slot + 1) * 128],
                                     in0=psS[:, 0:128], in1=Wp[:, 128 * m:128 * (m + 1)])
                        v.mark(v_w.h)
                    op("vector", fvw2)

                    nm2 = t_m2.bump()

                    def ftm2(t, s=s + frame_off, slot=slot, fi=fi, la=la, nvw=nvw,
                             nx1_prev=v_x1.n):
                        t.wait_ge(v_w.h, nvw)
                        if fi and nx1_prev >= 1:
                            t.wait_ge(v_x1.h, nx1_prev)
                        swk = sw[:, slot * 128:(slot + 1) * 128]
                        t.matmul(psA0[:], swk, V[:, D * s: D * s + 512],
                                 start=fi, stop=la)
                        t.matmul(psA1[:], swk, V[:, D * s + 512: D * s + 1024],
                                 start=fi, stop=la).then_inc(t_m2.h, 1)
                    op("tensor", ftm2)

                nx1 = v_x1.bump()

                def fvx1(v, par=par, nm2=t_m2.n, nld0=nld0, dsem=dsem,
                         ndo_prev=d_o[par].n):
                    v.wait_ge(t_m2.h, nm2)
                    v.wait_ge(dsem.h, nld0)
                    if ndo_prev >= 16:
                        v.wait_ge(d_o[par].h, ndo_prev)
                    x0 = xbuf[:, par * D:(par + 1) * D]
                    xt = x1t[:, par * D:(par + 1) * D]
                    v.scalar_tensor_tensor(out=xt[:, 0:512], in0=psA0[:],
                                           scalar=scalc[:, 1:2], in1=x0[:, 0:512],
                                           op0=ALU.mult, op1=ALU.add)
                    v.scalar_tensor_tensor(out=xt[:, 512:1024], in0=psA1[:],
                                           scalar=scalc[:, 1:2], in1=x0[:, 512:1024],
                                           op0=ALU.mult, op1=ALU.add)
                    v.mark(v_x1.h)
                op("vector", fvx1)

                ndo = d_o[par].bump(16)

                def fgo(g, par=par, nx1=nx1, row=xrow_of(q)):
                    g.wait_ge(v_x1.h, nx1)
                    g.dma_start(out=x1d[row:row + 128, :],
                                in_=x1t[:, par * D:(par + 1) * D]).then_inc(d_o[par].h, 16)
                op("gpsimd", fgo)

                nst = v_st.bump()

                def fvl(v, par=par):
                    xt = x1t[:, par * D:(par + 1) * D]
                    st = stats[:, par * 12:par * 12 + 12]
                    v.bn_stats(out=st[:, 0:6], in_=xt[:, 0:512])
                    v.bn_stats(out=st[:, 6:12], in_=xt[:, 512:1024])
                    v.bn_aggr(out=mv[:, par * 2:par * 2 + 2], in_=st)
                    v.tensor_scalar(out=veps[:, par:par + 1],
                                    in0=mv[:, par * 2 + 1:par * 2 + 2],
                                    scalar1=1e-5, scalar2=None, op0=ALU.add)
                    v.mark(v_st.h)
                op("vector", fvl)

                nrs = s_rs.bump()

                def fscl(s, par=par, nst=nst, nln_prev=v_ln.n):
                    s.wait_ge(v_st.h, nst)
                    if nln_prev >= 1:
                        s.wait_ge(v_ln.h, nln_prev)
                    s.activation(out=rstd[:, par:par + 1], in_=veps[:, par:par + 1], func=AF.Ln)
                    s.activation(out=rstd[:, par:par + 1], in_=rstd[:, par:par + 1],
                                 func=AF.Exp, scale=-0.5)
                    s.mark(s_rs.h)
                op("scalar", fscl)

                tr0 = t_tr.n
                nln = v_ln.bump()

                def fvn(v, par=par, nrs=nrs, tr0=tr0):
                    v.wait_ge(s_rs.h, nrs)
                    if tr0 > 0:
                        v.wait_ge(t_tr.h, tr0)
                    xt = x1t[:, par * D:(par + 1) * D]
                    v.tensor_scalar(out=lnt[:], in0=xt,
                                    scalar1=mv[:, par * 2:par * 2 + 1],
                                    scalar2=rstd[:, par:par + 1],
                                    op0=ALU.subtract, op1=ALU.mult)
                    v.mark(v_ln.h)
                op("vector", fvn)

                for ct in range(NCT):
                    ps, guard, g = rotT.acquire()
                    ntr = t_tr.bump()

                    def ftz(t, ct=ct, ps=ps, guard=guard, nln=nln):
                        if ct == 0:
                            t.wait_ge(v_ln.h, nln)
                        wait_guard(t, guard)
                        t.transpose(ps[:, 0:128], lnt[:, 128 * ct:128 * (ct + 1)],
                                    identt[:]).then_inc(t_tr.h, 1)
                    op("tensor", ftz)

                    nev = v_ev.bump()

                    def fvze(v, ct=ct, ps=ps, ntr=ntr):
                        v.wait_ge(t_tr.h, ntr)
                        v.tensor_copy(out=lnzT[:, 128 * ct:128 * (ct + 1)], in_=ps[:, 0:128])
                        v.mark(v_ev.h)
                    op("vector", fvze)
                    rotT.release(g, v_ev, nev)
                EVZ = v_ev.n

                ntz = t_z.bump()

                def ftzm(t, evz=EVZ, nz_prev=v_z.n):
                    t.wait_ge(v_ev.h, evz)
                    if nz_prev >= 1:
                        t.wait_ge(v_z.h, nz_prev)
                    for ct in range(NCT):
                        mi = t.matmul(psZ[:, 0:8], lnzT[:, 128 * ct:128 * (ct + 1)],
                                      projT_s[:, 8 * ct:8 * (ct + 1)],
                                      start=(ct == 0), stop=(ct == NCT - 1))
                    mi.then_inc(t_z.h, 1)
                op("tensor", ftzm)

                nvz = v_z.bump()

                def fvzz(v, zc=zcol_of(q), ntz=ntz):
                    v.wait_ge(t_z.h, ntz)
                    v.tensor_copy(out=z_sb[:, 8 * zc:8 * (zc + 1)], in_=psZ[:, 0:8])
                    v.mark(v_z.h)
                op("vector", fvzz)

        for bi in range(2):
            emit_attn_half(lambda i, bi=bi: 4096 * bi + 128 * i, 32,
                           lambda q, bi=bi: 32 * bi + q)
            emit_attn_half(lambda i, bi=bi: 4096 * bi + 2048 + 128 * i, 16,
                           lambda q, bi=bi: 32 * bi + 16 + q, frame_off=16)
        NVZ_R0 = v_z.n

        d_po = [S("d_po0"), S("d_po1")]

        def rest_runs(T0):
            if T0 == 0:
                return [(16, 40), (40, 64)]
            if T0 == 48:
                return [(0, 24), (24, 48)]
            return [(0, T0), (T0 + 16, 64)]

        def emit_exchange(rnd, vz_need):
            nsh0 = d_sh.n
            if exchange == "shared":
                d_sh.bump(32)

            def fex(g, rnd=rnd, vz_need=vz_need, nsh0=nsh0,
                    do0=d_o[0].n, do1=d_o[1].n):
                g.wait_ge(v_z.h, vz_need)
                if do0:
                    g.wait_ge(d_o[0].h, do0)
                if do1:
                    g.wait_ge(d_o[1].h, do1)
                if exchange == "shared":
                    for c in arms(g):
                        p = c % 2
                        g.dma_start(out=shz[128 * p:128 * (p + 1), 512 * rnd:512 * (rnd + 1)],
                                    in_=z_sb[:]).then_inc(d_sh.h, 16)
                    g.wait_ge(d_sh.h, nsh0 + 16)
                    for c in arms(g):
                        p = c % 2
                        g.dma_start(out=shflag[p:p + 1, rnd:rnd + 1],
                                    in_=nt_s[:, 0:1]).then_inc(d_sh.h, 16)
                    g.wait_ge(d_sh.h, nsh0 + 32)
                    with g.register(f"pc{rnd}") as creg, g.register(f"pw{rnd}") as wreg:
                        g.reg_mov(creg, 1)
                        g.reg_mov(wreg, 16)
                        with g.While(creg):
                            for c in arms(g):
                                qpar = 1 - (c % 2)
                                g.dma_start(out=ft_s[:, 0:1],
                                            in_=shflag[qpar:qpar + 1, rnd:rnd + 1]
                                            ).then_inc(d_po[rnd].h, 16)
                            g.wait_ge(d_po[rnd].h, wreg)
                            g.reg_alu(wreg, wreg, 16, op=ALU.add)
                            vv = g.value_load(ft_s[0:1, 0:1])
                            nn = g.value_load(nt_s[0:1, 0:1])
                            g.reg_alu(creg, vv, nn, op=ALU.not_equal)
                    for c in arms(g):
                        qpar = 1 - (c % 2)
                        g.dma_start(out=zsb[:, 512:1024],
                                    in_=shz[128 * qpar:128 * (qpar + 1),
                                            512 * rnd:512 * (rnd + 1)]).then_inc(d_z.h, 16)
                else:
                    src = pz0 if rnd == 0 else pz1
                    g.dma_start(out=zsb[:, 512:1024], in_=src).then_inc(d_z.h, 16)
                if rnd == 0:
                    g.dma_start(out=zsb[:, 0:512], in_=z_sb[:]).then_inc(d_z.h, 16)
                else:
                    for c in arms(g):
                        T0 = core_T0(c)
                        g.dma_start(out=zsb[:, 0:128],
                                    in_=z_sb[:, 8 * T0:8 * T0 + 128]).then_inc(d_z.h, 16)
                    for run in range(2):
                        for c in arms(g):
                            T0 = core_T0(c)
                            runs = rest_runs(T0)
                            off = 128 + 8 * (runs[0][1] - runs[0][0]) * run
                            a, b = runs[run]
                            g.dma_start(out=zsb[:, off:off + 8 * (b - a)],
                                        in_=z_sb[:, 8 * a:8 * b]).then_inc(d_z.h, 16)
            op("gpsimd", fex)
            nblocks = 2 if rnd == 0 else 4
            return d_z.bump(16 * nblocks)

        def emit_router(xr_need):
            nzb = v_zb.bump()

            def fvr(v, xr_need=xr_need):
                v.wait_ge(d_z.h, xr_need)
                for zt in range(NZT):
                    v.tensor_add(out=sall[:, 8 * zt:8 * (zt + 1)],
                                 in0=zsb[:, 8 * zt:8 * (zt + 1)], in1=pb_s[:])
                v.mark(v_zb.h)
            op("vector", fvr)
            nsg = s_sg.bump()

            def fsr(s, nzb=nzb):
                s.wait_ge(v_zb.h, nzb)
                s.activation(out=zsb[:], in_=sall[:], func=AF.Sigmoid)
                s.mark(s_sg.h)
            op("scalar", fsr)

            def fvr2(v, nsg=nsg):
                v.wait_ge(s_sg.h, nsg)
                for zt in range(NZT):
                    v.tensor_add(out=sall[:, 8 * zt:8 * (zt + 1)],
                                 in0=zsb[:, 8 * zt:8 * (zt + 1)], in1=eb_s[:])
                v.tensor_scalar(out=sall[:], in0=sall[:], scalar1=1e-12, scalar2=None,
                                op0=ALU.max)
            op("vector", fvr2)

            for zt in range(NZT):
                def fvr3(v, zt=zt):
                    sl = sall[:, 8 * zt:8 * (zt + 1)]
                    v.max(out=m8[:], in_=sl)
                    v.max_index(out=i8[:], in_max=m8[:], in_values=sl)
                    v.tensor_copy(out=egrid[:, 2 * zt:2 * zt + 2], in_=i8[:, 0:2])
                    v.tensor_add(out=rtmp[:, 0:1], in0=m8[:, 0:1], in1=m8[:, 1:2])
                    v.tensor_scalar(out=rtmp[:, 1:2], in0=rtmp[:, 0:1], scalar1=1e-12,
                                    scalar2=None, op0=ALU.max)
                    v.reciprocal(out=rtmp[:, 2:3], in_=rtmp[:, 1:2])
                    v.tensor_mul(out=wgrid[:, 2 * zt:2 * zt + 1], in0=m8[:, 0:1],
                                 in1=rtmp[:, 2:3])
                    v.tensor_mul(out=wgrid[:, 2 * zt + 1:2 * zt + 2], in0=m8[:, 1:2],
                                 in1=rtmp[:, 2:3])
                op("vector", fvr3)
            ngd = v_gd.bump()

            def fvr4(v):
                v.mark(v_gd.h)
            op("vector", fvr4)
            return v_gd.n

        def emit_bisect(ngd):
            n_gr1 = d_gr.bump(32)

            def fgb(g, ngd=ngd, n_gr0=n_gr1 - 32):
                g.wait_ge(v_gd.h, ngd)
                g.dma_start(out=wdram, in_=wgrid[:]).then_inc(d_gr.h, 16)
                g.dma_start(out=edram, in_=egrid[:]).then_inc(d_gr.h, 16)
            op("gpsimd", fgb)
            n_gr2 = d_gr.bump(16 * 16)

            def fgb2(g, n_gr1=n_gr1, do0=d_o[0].n, do1=d_o[1].n,
                     dx0=d_x[0].n, dx1=d_x[1].n):
                g.wait_ge(d_gr.h, n_gr1)
                if do0:
                    g.wait_ge(d_o[0].h, do0)
                if do1:
                    g.wait_ge(d_o[1].h, do1)
                if dx0:
                    g.wait_ge(d_x[0].h, dx0)
                if dx1:
                    g.wait_ge(d_x[1].h, dx1)
                for e in range(8):
                    g.dma_start(out=wge[16 * e:16 * (e + 1), 0:2048],
                                in_=wdram.rearrange("(a b) f -> a (b f)", a=16)
                                ).then_inc(d_gr.h, 16)
                    g.dma_start(out=epe[16 * e:16 * (e + 1), 0:2048],
                                in_=edram.rearrange("(a b) f -> a (b f)", a=16)
                                ).then_inc(d_gr.h, 16)
            op("gpsimd", fgb2)

            nit0 = v_it.bump()

            def fvb(v, n_gr2=n_gr2):
                v.wait_ge(d_gr.h, n_gr2)
                v.tensor_scalar(out=epe[:, 0:2048], in0=epe[:, 0:2048], scalar1=ecol_s[:],
                                scalar2=None, op0=ALU.is_equal)
                v.tensor_scalar(out=wge[:, 0:2048], in0=wge[:, 0:2048], scalar1=1.0,
                                scalar2=None, op0=ALU.add)
                v.tensor_mul(out=wge[:, 0:2048], in0=wge[:, 0:2048], in1=epe[:, 0:2048])
                v.tensor_scalar(out=wge[:, 0:2048], in0=wge[:, 0:2048], scalar1=1.0,
                                scalar2=None, op0=ALU.subtract)
                v.memset(lohi[:, 0:8], 0.0)
                v.memset(lohi[:, 2:3], 1.0)
                v.memset(lohi[:, 4:5], 0.5)
                v.mark(v_it.h)
            op("vector", fvb)

            for k in range(NITER):
                nthr = t_thr.bump()

                def ftb1(t, req=v_it.n):
                    t.wait_ge(v_it.h, req)
                    t.matmul(psB[:, 0:1], eselT_s[:], lohi[:, 4:5],
                             start=True, stop=True).then_inc(t_thr.h, 1)
                op("tensor", ftb1)

                nv1 = v_it.bump()

                def fvb1(v, nthr=nthr):
                    v.wait_ge(t_thr.h, nthr)
                    v.tensor_scalar(out=epe[:, 0:2048], in0=wge[:, 0:2048],
                                    scalar1=psB[:, 0:1], scalar2=None,
                                    op0=ALU.is_gt, op1=ALU.add, accum_out=cnt_col[:])
                    v.mark(v_it.h)
                op("vector", fvb1)

                ncnt = t_cnt.bump()

                def ftb2(t, nv1=nv1):
                    t.wait_ge(v_it.h, nv1)
                    t.matmul(psB[0:8, 0:1], esel_s[:], cnt_col[:],
                             start=True, stop=True).then_inc(t_cnt.h, 1)
                op("tensor", ftb2)

                nv2 = v_it.bump()

                def fvb2(v, k=k, ncnt=ncnt):
                    v.wait_ge(t_cnt.h, ncnt)
                    lo = lohi[:, k % 2:k % 2 + 1]
                    lon = lohi[:, (k + 1) % 2:(k + 1) % 2 + 1]
                    hi = lohi[:, 2 + k % 2:3 + k % 2]
                    hin = lohi[:, 2 + (k + 1) % 2:3 + (k + 1) % 2]
                    ge = lohi[:, 5:6]
                    gn = lohi[:, 6:7]
                    tmp = lohi[:, 7:8]
                    mid = lohi[:, 4:5]
                    v.tensor_scalar(out=ge, in0=psB[0:8, 0:1], scalar1=CAP,
                                    scalar2=None, op0=ALU.is_ge)
                    v.tensor_scalar(out=gn, in0=ge, scalar1=-1.0, scalar2=1.0,
                                    op0=ALU.mult, op1=ALU.add)
                    v.tensor_mul(out=tmp, in0=ge, in1=mid)
                    v.tensor_mul(out=lon, in0=gn, in1=lo)
                    v.tensor_add(out=lon, in0=lon, in1=tmp)
                    v.tensor_mul(out=tmp, in0=ge, in1=hi)
                    v.tensor_mul(out=hin, in0=gn, in1=mid)
                    v.tensor_add(out=hin, in0=hin, in1=tmp)
                    v.tensor_add(out=tmp, in0=lon, in1=hin)
                    v.tensor_scalar(out=mid, in0=tmp, scalar1=0.5, scalar2=None,
                                    op0=ALU.mult)
                    v.mark(v_it.h)
                op("vector", fvb2)

            fin = NITER % 2
            nx8v = v_it.bump()

            def fvx(v, fin=fin):
                v.tensor_scalar(out=X8[:], in0=ident8_s[:], scalar1=lohi[:, fin:fin + 1],
                                scalar2=None, op0=ALU.mult)
                v.mark(v_it.h)
            op("vector", fvx)
            nx8 = t_x8.bump()

            def ftx(t, nx8v=nx8v):
                t.wait_ge(v_it.h, nx8v)
                t.matmul(psB[:, 0:8], ones8x128_s[:], X8[:],
                         start=True, stop=True).then_inc(t_x8.h, 1)
            op("tensor", ftx)

            nfin = v_it.bump()

            def fvx2(v, nx8=nx8):
                v.wait_ge(t_x8.h, nx8)
                v.tensor_copy(out=lo_bc[:], in_=psB[:, 0:8])
                v.mark(v_it.h)
            op("vector", fvx2)
            return v_it.n

        def emit_gates(ntiles, tiles_off=0):
            for k in range(ntiles):
                ti = tiles_off + k
                ngt = v_gt.bump()

                def fvg(v, ti=ti):
                    v.tensor_scalar(out=geq[:, 0:8], in0=iota8_s[:],
                                    scalar1=egrid[:, 2 * ti:2 * ti + 1],
                                    scalar2=None, op0=ALU.is_equal)
                    v.tensor_scalar(out=geq[:, 8:16], in0=iota8_s[:],
                                    scalar1=egrid[:, 2 * ti + 1:2 * ti + 2],
                                    scalar2=None, op0=ALU.is_equal)
                    v.tensor_mul(out=rtmp[:, 8:16], in0=geq[:, 0:8], in1=lo_bc[:])
                    v.reduce_sum(out=rtmp[:, 0:1], in_=rtmp[:, 8:16], axis=AX.X)
                    v.tensor_mul(out=rtmp[:, 8:16], in0=geq[:, 8:16], in1=lo_bc[:])
                    v.reduce_sum(out=rtmp[:, 1:2], in_=rtmp[:, 8:16], axis=AX.X)
                    v.tensor_tensor(out=rtmp[:, 2:3], in0=wgrid[:, 2 * ti:2 * ti + 1],
                                    in1=rtmp[:, 0:1], op=ALU.is_gt)
                    v.tensor_tensor(out=rtmp[:, 3:4], in0=wgrid[:, 2 * ti + 1:2 * ti + 2],
                                    in1=rtmp[:, 1:2], op=ALU.is_gt)
                    v.tensor_mul(out=rtmp[:, 4:5], in0=wgrid[:, 2 * ti:2 * ti + 1],
                                 in1=rtmp[:, 2:3])
                    v.tensor_mul(out=rtmp[:, 5:6], in0=wgrid[:, 2 * ti + 1:2 * ti + 2],
                                 in1=rtmp[:, 3:4])
                    v.tensor_scalar(out=geq[:, 0:8], in0=geq[:, 0:8], scalar1=rtmp[:, 4:5],
                                    scalar2=None, op0=ALU.mult)
                    v.tensor_scalar(out=geq[:, 8:16], in0=geq[:, 8:16], scalar1=rtmp[:, 5:6],
                                    scalar2=None, op0=ALU.mult)
                    v.tensor_add(out=gtm[:], in0=geq[:, 0:8], in1=geq[:, 8:16])
                    v.mark(v_gt.h)
                op("vector", fvg)

                ps, guard, g = rotT.acquire()
                ntr = t_tr.bump()

                def ftg(t, ps=ps, guard=guard, ngt=ngt):
                    t.wait_ge(v_gt.h, ngt)
                    wait_guard(t, guard)
                    t.transpose(ps[0:8, 0:128], gtm[:], identt[:]).then_inc(t_tr.h, 1)
                op("tensor", ftg)

                ng8 = v_g8.bump()

                def fvg8(v, k=k, ps=ps, ntr=ntr):
                    v.wait_ge(t_tr.h, ntr)
                    v.tensor_copy(out=g8all[:, 128 * k:128 * (k + 1)], in_=ps[0:8, 0:128])
                    v.mark(v_g8.h)
                op("vector", fvg8)
                rotT.release(g, v_g8, ng8)
            return v_g8.n

        wslot_guard = {}
        hm_guard = [None]

        def emit_ln_tile(row_fn, cm_dst_col, first_waits):
            ti_par = emit_ln_tile.par
            emit_ln_tile.par ^= 1
            par = ti_par
            dsem = d_x[par]
            nld = dsem.bump(16)

            def fs(sy, par=par, dsem=dsem, row_fn=row_fn, fw=first_waits,
                   nln_prev=v_ln.n):
                for sem, val in fw:
                    sy.wait_ge(sem, val)
                if not fw and nln_prev >= 2:
                    sy.wait_ge(v_ln.h, nln_prev - 1)
                for c in (arms(sy) if callable(row_fn) else [0]):
                    row = row_fn(c) if callable(row_fn) else row_fn
                    sy.dma_start(out=xbuf[:, par * D:(par + 1) * D],
                                 in_=(x3d if callable(row_fn) else x1d)[row:row + 128, :]
                                 ).then_inc(dsem.h, 16)
            op("sync", fs)

            nst = v_st.bump()

            def fv1(v, par=par, nld=nld, dsem=dsem):
                v.wait_ge(dsem.h, nld)
                xb = xbuf[:, par * D:(par + 1) * D]
                st = stats[:, par * 12:par * 12 + 12]
                v.bn_stats(out=st[:, 0:6], in_=xb[:, 0:512])
                v.bn_stats(out=st[:, 6:12], in_=xb[:, 512:1024])
                v.bn_aggr(out=mv[:, par * 2:par * 2 + 2], in_=st)
                v.tensor_scalar(out=veps[:, par:par + 1],
                                in0=mv[:, par * 2 + 1:par * 2 + 2],
                                scalar1=1e-5, scalar2=None, op0=ALU.add)
                v.mark(v_st.h)
            op("vector", fv1)

            nrs = s_rs.bump()

            def fsc(s, par=par, nst=nst, nln_prev=v_ln.n):
                s.wait_ge(v_st.h, nst)
                if nln_prev >= 1:
                    s.wait_ge(v_ln.h, nln_prev)
                s.activation(out=rstd[:, par:par + 1], in_=veps[:, par:par + 1], func=AF.Ln)
                s.activation(out=rstd[:, par:par + 1], in_=rstd[:, par:par + 1],
                             func=AF.Exp, scale=-0.5)
                s.mark(s_rs.h)
            op("scalar", fsc)

            tr0 = t_tr.n
            nln = v_ln.bump()

            def fv2(v, par=par, nrs=nrs, tr0=tr0):
                v.wait_ge(s_rs.h, nrs)
                if tr0 > 0:
                    v.wait_ge(t_tr.h, tr0)
                xb = xbuf[:, par * D:(par + 1) * D]
                v.tensor_scalar(out=lnt[:], in0=xb,
                                scalar1=mv[:, par * 2:par * 2 + 1],
                                scalar2=rstd[:, par:par + 1],
                                op0=ALU.subtract, op1=ALU.mult)
                v.mark(v_ln.h)
            op("vector", fv2)

            for ct in range(NCT):
                ps, guard, g = rotT.acquire()
                ntr = t_tr.bump()

                def ft(t, ct=ct, ps=ps, guard=guard, nln=nln):
                    if ct == 0:
                        t.wait_ge(v_ln.h, nln)
                    wait_guard(t, guard)
                    t.transpose(ps[:, 0:128], lnt[:, 128 * ct:128 * (ct + 1)],
                                identt[:]).then_inc(t_tr.h, 1)
                op("tensor", ft)

                nev = v_ev.bump()

                def fv3(v, ct=ct, ps=ps, ntr=ntr, dst_col=cm_dst_col):
                    v.wait_ge(t_tr.h, ntr)
                    v.tensor_copy(out=cmA[:, 2048 * ct + dst_col: 2048 * ct + dst_col + 128],
                                  in_=ps[:, 0:128])
                    v.mark(v_ev.h)
                op("vector", fv3)
                rotT.release(g, v_ev, nev)
        emit_ln_tile.par = 0

        def emit_ffn_segment(kind, seg, vit_fin, ng8, extra_w_guards=()):
            for ti in range(16):
                fw = []
                if ti == 0 and seg in (0, 4):
                    fw.append((v_it.h, vit_fin))
                if ti < 2:
                    for sm in (v_x2, v_x3):
                        if sm.n > 0:
                            fw.append((sm.h, sm.n))
                if kind == "p3":
                    emit_ln_tile(2048 * seg + 128 * ti, 128 * ti, fw)
                else:
                    emit_ln_tile(lambda c, ti=ti: 128 * (core_T0(c) + ti),
                                 128 * ti, fw)
            LNEV = v_ev.n

            for ui, unit in enumerate(UNITS):
                kindu = unit[0]
                b = ui % 2
                nw1 = d_w1[b].bump(16)
                nw2 = d_w2[b].bump(16)
                uguard = wslot_guard.get(("u", b))

                def fgw_b(g, ui=ui, b=b, uguard=uguard,
                          extra=(tuple(extra_w_guards) if ui < 2 else ())):
                    if uguard is not None:
                        g.wait_ge(uguard[0].h, uguard[1])
                    for sem, val in extra:
                        g.wait_ge(sem, val)
                    g.dma_start(out=fc1sb[:, b * 8192:(b + 1) * 8192].rearrange(
                                    "p (ct m) -> p ct m", m=1024),
                                in_=u1wb[ui].rearrange("(ct p) m -> p ct m", p=128)
                                ).then_inc(d_w1[b].h, 16)
                    g.dma_start(out=fc2sb[:, b * 8192:(b + 1) * 8192].rearrange(
                                    "p (mt c) -> p mt c", c=1024),
                                in_=u2wb[ui].rearrange("(mt p) c -> p mt c", p=128)
                                ).then_inc(d_w2[b].h, 16)
                op("gpsimd", fgw_b)

                for ci in range(4):
                    cst = 512 * ci
                    if kindu == "ex":
                        e = unit[1]
                        psg, guard, gg = rotA.acquire()
                        ngmm = t_mm.bump()

                        def ftgate(t, e=e, psg=psg, guard=guard, cst=cst,
                                   first=(ui == 4 and ci == 0), ng8=ng8):
                            if first:
                                t.wait_ge(v_g8.h, ng8)
                            wait_guard(t, guard)
                            t.matmul(psg[:, 0:512], sel_s[:, 128 * e:128 * (e + 1)],
                                     g8all[:, cst:cst + 512],
                                     start=True, stop=True).then_inc(t_mm.h, 1)
                        op("tensor", ftgate)

                        ngc = v_gc.bump()

                        def fvgate(v, psg=psg, ngmm=ngmm, ci=ci):
                            v.wait_ge(t_mm.h, ngmm)
                            v.tensor_copy(out=gate_sb[:, (ci % 2) * 512:(ci % 2) * 512 + 512],
                                          in_=psg[:, 0:512])
                            v.mark(v_gc.h)
                        op("vector", fvgate)
                        rotA.release(gg, v_gc, ngc)

                    ty_chunk_prev = t_y.n
                    for mt in range(NCT):
                        psm, guard, gg = rotA.acquire()
                        nmm = t_mm.bump()

                        def ftf1(t, ui=ui, b=b, mt=mt, psm=psm, guard=guard,
                                 cst=cst, nw1=nw1, nw2=nw2, ci=ci,
                                 first=(ui == 0 and ci == 0), lnev=LNEV):
                            if ci == 0 and mt == 0:
                                t.wait_ge(d_w1[b].h, nw1)
                                t.wait_ge(d_w2[b].h, nw2)
                                if first:
                                    t.wait_ge(v_ev.h, lnev)
                            wait_guard(t, guard)
                            for ct in range(NCT):
                                mi = t.matmul(psm[:, 0:512],
                                              fc1sb[:, b * 8192 + ct * 1024 + 128 * mt:
                                                    b * 8192 + ct * 1024 + 128 * (mt + 1)],
                                              cmA[:, 2048 * ct + cst: 2048 * ct + cst + 512],
                                              start=(ct == 0), stop=(ct == NCT - 1))
                            mi.then_inc(t_mm.h, 1)
                        op("tensor", ftf1)

                        nsh = s_h.bump()
                        if kindu == "sh":
                            hdst = hbuf[:, mt * 512:mt * 512 + 512]
                        else:
                            hdst = hraw[:, (mt % 2) * 512:(mt % 2) * 512 + 512]

                        def fsg(s, psm=psm, nmm=nmm, hdst=hdst, ui=ui, mt=mt,
                                kindu=kindu, prev_vh=v_h.n, typrev=ty_chunk_prev):
                            s.wait_ge(t_mm.h, nmm)
                            if kindu == "sh" and mt == 0 and typrev > 0:
                                s.wait_ge(t_y.h, typrev)
                            if kindu == "ex" and prev_vh >= 2:
                                s.wait_ge(v_h.h, prev_vh - 1)
                            s.activation(out=hdst, in_=psm[:, 0:512], func=AF.Gelu,
                                         bias=ubias_s[:, ui * 8 + mt:ui * 8 + mt + 1],
                                         scale=1.0)
                            s.mark(s_h.h)
                        op("scalar", fsg)
                        rotA.release(gg, s_h, nsh)

                        if kindu == "ex":
                            nvh = v_h.bump()

                            def fvh(v, mt=mt, ci=ci, nsh=nsh, ngc=v_gc.n,
                                    typrev=ty_chunk_prev):
                                v.wait_ge(s_h.h, nsh)
                                v.wait_ge(v_gc.h, ngc)
                                if mt == 0 and typrev > 0:
                                    v.wait_ge(t_y.h, typrev)
                                v.tensor_mul(out=hbuf[:, mt * 512:mt * 512 + 512],
                                             in0=hraw[:, (mt % 2) * 512:(mt % 2) * 512 + 512],
                                             in1=gate_sb[:, (ci % 2) * 512:(ci % 2) * 512 + 512])
                                v.mark(v_h.h)
                            op("vector", fvh)

                    h_sem, h_n = (s_h, s_h.n) if kindu == "sh" else (v_h, v_h.n)
                    for ct in range(NCT):
                        psv, guardY, gy = rotY.acquire()
                        nty = t_y.bump()

                        def ftf2(t, ui=ui, b=b, ct=ct, cst=cst, psv=psv,
                                 guardY=guardY, h_sem=h_sem, h_n=h_n):
                            if ct == 0:
                                t.wait_ge(h_sem.h, h_n)
                            wait_guard(t, guardY)
                            for mt in range(NCT):
                                mi = t.matmul(psv[:, 0:512],
                                              fc2sb[:, b * 8192 + mt * 1024 + 128 * ct:
                                                    b * 8192 + mt * 1024 + 128 * (ct + 1)],
                                              hbuf[:, mt * 512:mt * 512 + 512],
                                              start=(mt == 0), stop=(mt == NCT - 1))
                            mi.then_inc(t_y.h, 1)
                        op("tensor", ftf2)

                        nvy = v_y.bump()

                        def fvy(v, ui=ui, ct=ct, cst=cst, psv=psv, nty=nty):
                            v.wait_ge(t_y.h, nty)
                            dst = y_cm[:, 2048 * ct + cst: 2048 * ct + cst + 512]
                            if ui == 0:
                                v.tensor_copy(out=dst, in_=psv[:, 0:512])
                            else:
                                v.tensor_add(out=dst, in0=dst, in1=psv[:, 0:512])
                            v.mark(v_y.h)
                        op("vector", fvy)
                        rotY.release(gy, v_y, nvy)
                wslot_guard[("u", b)] = (t_y, t_y.n)

            for ci in range(4):
                cst = 512 * ci
                for ct in range(NCT):
                    psv, guardY, gy = rotY.acquire()
                    nty = t_y.bump()

                    def ftfb(t, ct=ct, cst=cst, psv=psv, guardY=guardY):
                        wait_guard(t, guardY)
                        t.matmul(psv[:, 0:512], fc2bT_s[0:8, 128 * ct:128 * (ct + 1)],
                                 g8all[:, cst:cst + 512], start=True, stop=False)
                        t.matmul(psv[:, 0:512], sh2b_s[0:1, 128 * ct:128 * (ct + 1)],
                                 ones512_s[0:1, 0:512], start=False,
                                 stop=True).then_inc(t_y.h, 1)
                    op("tensor", ftfb)

                    nvy = v_y.bump()

                    def fvyb(v, ct=ct, cst=cst, psv=psv, nty=nty):
                        v.wait_ge(t_y.h, nty)
                        dst = y_cm[:, 2048 * ct + cst: 2048 * ct + cst + 512]
                        v.tensor_add(out=dst, in0=dst, in1=psv[:, 0:512])
                        v.mark(v_y.h)
                    op("vector", fvyb)
                    rotY.release(gy, v_y, nvy)
                VY_CHUNK = v_y.n

                for sub in range(4):
                    ti = 4 * ci + sub
                    par = ti % 2
                    dsem = d_x[par]
                    nld = dsem.bump(16)

                    def fsr2(sy, ti=ti, par=par, dsem=dsem, nvx_prev=v_x2.n,
                             kind=kind, seg=seg, first=(ci == 0 and sub == 0),
                             vitf=vit_fin, lnv=v_ln.n):
                        if first:
                            sy.wait_ge(v_ln.h, lnv)
                        if nvx_prev >= 2 * NCT:
                            sy.wait_ge(v_x2.h, nvx_prev - NCT)
                        if kind == "p3":
                            sy.dma_start(out=xbuf[:, par * D:(par + 1) * D],
                                         in_=x1d[2048 * seg + 128 * ti:
                                                 2048 * seg + 128 * (ti + 1), :]
                                         ).then_inc(dsem.h, 16)
                        else:
                            for c in arms(sy):
                                row = 128 * (core_T0(c) + ti)
                                sy.dma_start(out=xbuf[:, par * D:(par + 1) * D],
                                             in_=x3d[row:row + 128, :]).then_inc(dsem.h, 16)
                    op("sync", fsr2)

                    ndo_prev = d_o[par].n
                    for ct in range(NCT):
                        ps, guard, g = rotT.acquire()
                        ntr = t_tr.bump()

                        def ftt(t, ct=ct, ti=ti, ps=ps, guard=guard, nvy=VY_CHUNK):
                            if ct == 0:
                                t.wait_ge(v_y.h, nvy)
                            wait_guard(t, guard)
                            t.transpose(ps[:, 0:128],
                                        y_cm[:, 2048 * ct + 128 * ti: 2048 * ct + 128 * (ti + 1)],
                                        identt[:]).then_inc(t_tr.h, 1)
                        op("tensor", ftt)

                        nvx = v_x2.bump()

                        def fvx2t(v, ct=ct, ps=ps, ntr=ntr, par=par, nld=nld,
                                  dsem=dsem, ndo_prev=ndo_prev):
                            if ct == 0:
                                v.wait_ge(dsem.h, nld)
                                if ndo_prev >= 16:
                                    v.wait_ge(d_o[par].h, ndo_prev)
                            v.wait_ge(t_tr.h, ntr)
                            v.tensor_add(out=x2t[:, par * D + 128 * ct: par * D + 128 * (ct + 1)],
                                         in0=xbuf[:, par * D + 128 * ct: par * D + 128 * (ct + 1)],
                                         in1=ps[:, 0:128])
                            v.mark(v_x2.h)
                        op("vector", fvx2t)
                        rotT.release(g, v_x2, nvx)

                    ndo = d_o[par].bump(16)

                    def fgo2(g, ti=ti, par=par, nvx=v_x2.n, kind=kind, seg=seg):
                        g.wait_ge(v_x2.h, nvx)
                        dst = (x2d[2048 * seg + 128 * ti: 2048 * seg + 128 * (ti + 1), :]
                               if kind == "p3" else y_out[128 * ti:128 * (ti + 1), :])
                        g.dma_start(out=dst, in_=x2t[:, par * D:(par + 1) * D]
                                    ).then_inc(d_o[par].h, 16)
                    op("gpsimd", fgo2)

        def emit_attn_b(bi, ty_p3, tmm_p3, vit_fin0):
            R = 4096 * bi
            if bi == 0:
                nwf = d_wf.bump(48)

                def fwf(sy, ty_p3=ty_p3, tmm_p3=tmm_p3, vitf=vit_fin0):
                    sy.wait_ge(t_y.h, ty_p3)
                    sy.wait_ge(t_mm.h, tmm_p3)
                    sy.wait_ge(v_it.h, vitf)
                    sy.dma_start(out=wfix_in.rearrange("p (ct m) -> p ct m", m=1024),
                                 in_=in_wb.rearrange("(ct p) m -> p ct m", p=128)
                                 ).then_inc(d_wf.h, 16)
                    sy.dma_start(out=wfix_out.rearrange("p (ct m) -> p ct m", m=1024),
                                 in_=out_wb.rearrange("(ct p) m -> p ct m", p=128)
                                 ).then_inc(d_wf.h, 16)
                    sy.dma_start(out=kern_s, in_=kern_cm).then_inc(d_wf.h, 16)
                op("sync", fwf)
                wslot_guard[("w", 0)] = (t_y, ty_p3)
                wslot_guard[("w", 1)] = (t_y, ty_p3)
            NWF = d_wf.n

            for cj in range(8):
                cst = 512 * cj
                for k in range(4):
                    ti5 = 4 * cj + k
                    par = emit_ln_tile.par
                    emit_ln_tile.par ^= 1
                    dsem = d_x[par]
                    nld = dsem.bump(16)

                    def fs5(sy, ti5=ti5, par=par, dsem=dsem, R=R,
                            first=(cj == 0 and k == 0), nln_prev=v_ln.n,
                            nvx2=v_x2.n, nvx3=v_x3.n):
                        if first:
                            if nvx2 > 0:
                                sy.wait_ge(v_x2.h, nvx2)
                            if nvx3 > 0:
                                sy.wait_ge(v_x3.h, nvx3)
                        if nln_prev >= 2:
                            sy.wait_ge(v_ln.h, nln_prev - 1)
                        sy.dma_start(out=xbuf[:, par * D:(par + 1) * D],
                                     in_=x2d[R + 128 * ti5:R + 128 * (ti5 + 1), :]
                                     ).then_inc(dsem.h, 16)
                    op("sync", fs5)

                    nst = v_st.bump()

                    def fv51(v, par=par, nld=nld, dsem=dsem):
                        v.wait_ge(dsem.h, nld)
                        xb = xbuf[:, par * D:(par + 1) * D]
                        st = stats[:, par * 12:par * 12 + 12]
                        v.bn_stats(out=st[:, 0:6], in_=xb[:, 0:512])
                        v.bn_stats(out=st[:, 6:12], in_=xb[:, 512:1024])
                        v.bn_aggr(out=mv[:, par * 2:par * 2 + 2], in_=st)
                        v.tensor_scalar(out=veps[:, par:par + 1],
                                        in0=mv[:, par * 2 + 1:par * 2 + 2],
                                        scalar1=1e-5, scalar2=None, op0=ALU.add)
                        v.mark(v_st.h)
                    op("vector", fv51)

                    nrs = s_rs.bump()

                    def fs5c(sc, par=par, nst=nst, nln_prev=v_ln.n):
                        sc.wait_ge(v_st.h, nst)
                        if nln_prev >= 1:
                            sc.wait_ge(v_ln.h, nln_prev)
                        sc.activation(out=rstd[:, par:par + 1], in_=veps[:, par:par + 1],
                                      func=AF.Ln)
                        sc.activation(out=rstd[:, par:par + 1], in_=rstd[:, par:par + 1],
                                      func=AF.Exp, scale=-0.5)
                        sc.mark(s_rs.h)
                    op("scalar", fs5c)

                    tr0 = t_tr.n
                    nln = v_ln.bump()

                    def fv52(v, par=par, nrs=nrs, tr0=tr0):
                        v.wait_ge(s_rs.h, nrs)
                        if tr0 > 0:
                            v.wait_ge(t_tr.h, tr0)
                        xb = xbuf[:, par * D:(par + 1) * D]
                        v.tensor_scalar(out=lnt[:], in0=xb,
                                        scalar1=mv[:, par * 2:par * 2 + 1],
                                        scalar2=rstd[:, par:par + 1],
                                        op0=ALU.subtract, op1=ALU.mult)
                        v.mark(v_ln.h)
                    op("vector", fv52)

                    for ct in range(NCT):
                        ps, guard, g = rotT.acquire()
                        ntr = t_tr.bump()

                        def ft5(t, ct=ct, ps=ps, guard=guard, nln=nln):
                            if ct == 0:
                                t.wait_ge(v_ln.h, nln)
                            wait_guard(t, guard)
                            t.transpose(ps[:, 0:128], lnt[:, 128 * ct:128 * (ct + 1)],
                                        identt[:]).then_inc(t_tr.h, 1)
                        op("tensor", ft5)

                        nev = v_ev.bump()

                        def fv53(v, k=k, ct=ct, ps=ps, ntr=ntr):
                            v.wait_ge(t_tr.h, ntr)
                            v.tensor_copy(out=lnchunk[:, 512 * ct + 128 * k:
                                                      512 * ct + 128 * (k + 1)],
                                          in_=ps[:, 0:128])
                            v.mark(v_ev.h)
                        op("vector", fv53)
                        rotT.release(g, v_ev, nev)
                EV5 = v_ev.n

                for cto in range(NCT):
                    psA, guard, g = rotA.acquire()
                    nmm = t_mm.bump()

                    def ftxp(t, cto=cto, psA=psA, guard=guard, nwf=NWF, ev5=EV5,
                             first=(cto == 0)):
                        if first:
                            t.wait_ge(d_wf.h, nwf)
                            t.wait_ge(v_ev.h, ev5)
                        wait_guard(t, guard)
                        for cti in range(NCT):
                            mi = t.matmul(psA[:, 0:512],
                                          wfix_in[:, cti * 1024 + 128 * cto:
                                                  cti * 1024 + 128 * (cto + 1)],
                                          lnchunk[:, 512 * cti:512 * cti + 512],
                                          start=(cti == 0), stop=(cti == NCT - 1))
                        mi.then_inc(t_mm.h, 1)
                    op("tensor", ftxp)

                    nps = v_ps.bump()

                    def fvxp(v, cto=cto, psA=psA, nmm=nmm,
                             first=(cj == 0 and cto == 0),
                             do0=d_o[0].n, do1=d_o[1].n):
                        v.wait_ge(t_mm.h, nmm)
                        if first:
                            if do0 >= 16:
                                v.wait_ge(d_o[0].h, do0)
                            if do1 >= 16:
                                v.wait_ge(d_o[1].h, do1)
                        v.tensor_scalar(out=xproj[:, 512 * cto:512 * cto + 512],
                                        in0=psA[:, 0:512], scalar1=inb_s[:, cto:cto + 1],
                                        scalar2=None, op0=ALU.add)
                        v.mark(v_ps.h)
                    op("vector", fvxp)
                    rotA.release(g, v_ps, nps)

                for ht in range(32):
                    bwm = ht % 2
                    nwm = d_wm[bwm].bump(16)
                    wg = wslot_guard[("w", bwm)]

                    def fgw1(g, ht=ht, bwm=bwm, wg=wg):
                        g.wait_ge(wg[0].h, wg[1])
                        g.dma_start(out=wstream[:, bwm * 4096:bwm * 4096 + 1024].rearrange(
                                        "p (ct m) -> p ct m", m=128),
                                    in_=mod_w1b[:, 128 * ht:128 * (ht + 1)].rearrange(
                                        "(ct p) m -> p ct m", p=128)).then_inc(d_wm[bwm].h, 16)
                    op("sync", fgw1)

                    psA, guard, g = rotA.acquire()
                    nmm = t_mm.bump()
                    wslot_guard[("w", bwm)] = (t_mm, nmm)

                    def ftm1(t, ht=ht, bwm=bwm, psA=psA, guard=guard, nwm=nwm):
                        t.wait_ge(d_wm[bwm].h, nwm)
                        wait_guard(t, guard)
                        for cti in range(NCT):
                            mi = t.matmul(psA[:, 0:512],
                                          wstream[:, bwm * 4096 + 128 * cti:
                                                  bwm * 4096 + 128 * (cti + 1)],
                                          lnchunk[:, 512 * cti:512 * cti + 512],
                                          start=(cti == 0), stop=(cti == NCT - 1))
                        mi.then_inc(t_mm.h, 1)
                    op("tensor", ftm1)

                    nsh = s_h.bump()

                    def fsm1(sc, ht=ht, psA=psA, nmm=nmm, prev_hm=hm_guard[0]):
                        sc.wait_ge(t_mm.h, nmm)
                        if prev_hm is not None:
                            sc.wait_ge(t_mm.h, prev_hm)
                        sc.activation(out=hm[:, 512 * ht:512 * ht + 512],
                                      in_=psA[:, 0:512], func=AF.Gelu,
                                      bias=modb1_s[:, ht:ht + 1], scale=1.0)
                        sc.mark(s_h.h)
                    op("scalar", fsm1)
                    rotA.release(g, s_h, nsh)
                SH5 = s_h.n

                for cto in range(NCT):
                    bwm = cto % 2
                    nwm = d_wm[bwm].bump(16)
                    wg = wslot_guard[("w", bwm)]

                    def fgw2(g, cto=cto, bwm=bwm, wg=wg):
                        g.wait_ge(wg[0].h, wg[1])
                        g.dma_start(out=wstream[:, bwm * 4096:(bwm + 1) * 4096].rearrange(
                                        "p (ht m) -> p ht m", m=128),
                                    in_=mod_w2b[:, 128 * cto:128 * (cto + 1)].rearrange(
                                        "(ht p) m -> p ht m", p=128)).then_inc(d_wm[bwm].h, 16)
                    op("sync", fgw2)

                    psA, guard, g = rotA.acquire()
                    nmm = t_mm.bump()
                    wslot_guard[("w", bwm)] = (t_mm, nmm)

                    def ftm2(t, cto=cto, bwm=bwm, psA=psA, guard=guard, nwm=nwm,
                             sh5=SH5):
                        t.wait_ge(d_wm[bwm].h, nwm)
                        if cto == 0:
                            t.wait_ge(s_h.h, sh5)
                        wait_guard(t, guard)
                        for ht in range(32):
                            mi = t.matmul(psA[:, 0:512],
                                          wstream[:, bwm * 4096 + 128 * ht:
                                                  bwm * 4096 + 128 * (ht + 1)],
                                          hm[:, 512 * ht:512 * ht + 512],
                                          start=(ht == 0), stop=(ht == 31))
                        mi.then_inc(t_mm.h, 1)
                    op("tensor", ftm2)

                    nmo = v_mo.bump()

                    def fvm2(v, cto=cto, psA=psA, nmm=nmm, cst=cst):
                        v.wait_ge(t_mm.h, nmm)
                        v.tensor_scalar(out=sall[:, 0:512], in0=psA[:, 0:512],
                                        scalar1=modb2_s[:, cto:cto + 1], scalar2=None,
                                        op0=ALU.add)
                        v.tensor_mul(out=modu[:, 4160 * cto + 64 + cst:
                                             4160 * cto + 64 + cst + 512],
                                     in0=sall[:, 0:512],
                                     in1=xproj[:, 512 * cto:512 * cto + 512])
                        v.mark(v_mo.h)
                    op("vector", fvm2)
                    rotA.release(g, v_mo, nmo)
                hm_guard[0] = t_mm.n

            nmo_pad = v_mo.bump()

            def fvpad(v):
                for cto in range(NCT):
                    v.tensor_copy(out=modu[:, 4160 * cto:4160 * cto + 64],
                                  in_=modu[:, 4160 * cto + 4096:4160 * cto + 4160])
                v.mark(v_mo.h)
            op("vector", fvpad)
            NMO = v_mo.n
            TMM_MOD = t_mm.n

            for h in range(2):
                for ct in range(NCT):
                    nd = v_diag.bump()

                    def fvd(v, ct=ct, first=(h == 0 and ct == 0), tmm_mod=TMM_MOD,
                            tcv_prev=t_cv.n, nwf=NWF):
                        if first:
                            v.wait_ge(t_mm.h, tmm_mod)
                            v.wait_ge(d_wf.h, nwf)
                        if tcv_prev > 0:
                            v.wait_ge(t_cv.h, tcv_prev)
                        for j in range(64):
                            v.tensor_scalar(out=diag64[:, 128 * j:128 * (j + 1)],
                                            in0=identt[:],
                                            scalar1=kern_s[:, 64 * ct + j:64 * ct + j + 1],
                                            scalar2=None, op0=ALU.mult)
                        v.mark(v_diag.h)
                    op("vector", fvd)

                    for c4r in range(4):
                        c4 = 4 * h + c4r
                        psC, guardC, gC = rotY.acquire()
                        ncv = t_cv.bump()

                        def ftc(t, ct=ct, c4=c4, psC=psC, guardC=guardC, nd=nd,
                                nmo=NMO):
                            t.wait_ge(v_diag.h, nd)
                            if c4 == 0 or (c4 == 4 and ct == 0):
                                t.wait_ge(v_mo.h, nmo)
                            wait_guard(t, guardC)
                            base = 4160 * ct + 64 + 512 * c4
                            for j in range(64):
                                mi = t.matmul(psC[:, 0:512],
                                              diag64[:, 128 * j:128 * (j + 1)],
                                              modu[:, base - j: base - j + 512],
                                              start=(j == 0), stop=(j == 63))
                            mi.then_inc(t_cv.h, 1)
                        op("tensor", ftc)

                        ncs = v_cs.bump()

                        def fvc(v, ct=ct, c4r=c4r, psC=psC, ncv=ncv):
                            v.wait_ge(t_cv.h, ncv)
                            v.tensor_copy(out=conv_full[:, 2048 * ct + 512 * c4r:
                                                        2048 * ct + 512 * (c4r + 1)],
                                          in_=psC[:, 0:512])
                            v.mark(v_cs.h)
                        op("vector", fvc)
                        rotY.release(gC, v_cs, ncs)
                VCS_H0 = v_cs.n - 32

                for c4r in range(4):
                    for cto in range(NCT):
                        psA, guard, g = rotA.acquire()
                        nmm = t_mm.bump()

                        def fto(t, c4r=c4r, cto=cto, psA=psA, guard=guard,
                                needcs=(VCS_H0 + 7 * 4 + c4r + 1)):
                            t.wait_ge(v_cs.h, needcs)
                            wait_guard(t, guard)
                            for cti in range(NCT):
                                mi = t.matmul(psA[:, 0:512],
                                              wfix_out[:, cti * 1024 + 128 * cto:
                                                       cti * 1024 + 128 * (cto + 1)],
                                              conv_full[:, 2048 * cti + 512 * c4r:
                                                        2048 * cti + 512 * (c4r + 1)],
                                              start=(cti == 0), stop=(cti == NCT - 1))
                            mi.then_inc(t_mm.h, 1)
                        op("tensor", fto)

                        nps = v_ps.bump()

                        def fvo(v, cto=cto, psA=psA, nmm=nmm):
                            v.wait_ge(t_mm.h, nmm)
                            v.tensor_scalar(out=osb[:, 512 * cto:512 * (cto + 1)],
                                            in0=psA[:, 0:512],
                                            scalar1=outb_s[:, cto:cto + 1],
                                            scalar2=None, op0=ALU.add)
                            v.mark(v_ps.h)
                        op("vector", fvo)
                        rotA.release(g, v_ps, nps)
                    OSB_PS = v_ps.n

                    for st4 in range(4):
                        til = 16 * h + 4 * c4r + st4
                        par = emit_ln_tile.par
                        emit_ln_tile.par ^= 1
                        dsem = d_x[par]
                        nld = dsem.bump(16)

                        def fsx3(sy, til=til, par=par, dsem=dsem, R=R,
                                 nln_prev=v_ln.n, nx3_prev=v_x3.n):
                            sy.wait_ge(v_ln.h, nln_prev)
                            if nx3_prev >= 2:
                                sy.wait_ge(v_x3.h, nx3_prev - 1)
                            sy.dma_start(out=xbuf[:, par * D:(par + 1) * D],
                                         in_=x2d[R + 128 * til:R + 128 * (til + 1), :]
                                         ).then_inc(dsem.h, 16)
                        op("sync", fsx3)

                        for ct in range(NCT):
                            ps, guard, g = rotT.acquire()
                            ntr = t_tr.bump()

                            def ftt3(t, ct=ct, st4=st4, ps=ps, guard=guard,
                                     ops_n=OSB_PS):
                                if ct == 0:
                                    t.wait_ge(v_ps.h, ops_n)
                                wait_guard(t, guard)
                                t.transpose(ps[:, 0:128],
                                            osb[:, 512 * ct + 128 * st4:
                                                512 * ct + 128 * (st4 + 1)],
                                            identt[:]).then_inc(t_tr.h, 1)
                            op("tensor", ftt3)

                            nx3 = v_x3.bump()

                            def fvx3(v, ct=ct, ps=ps, ntr=ntr, par=par, nld=nld,
                                     dsem=dsem, ndo_prev=d_o[par].n):
                                if ct == 0:
                                    v.wait_ge(dsem.h, nld)
                                    if ndo_prev >= 16:
                                        v.wait_ge(d_o[par].h, ndo_prev)
                                v.wait_ge(t_tr.h, ntr)
                                v.tensor_add(out=x3t[:, par * D + 128 * ct:
                                                     par * D + 128 * (ct + 1)],
                                             in0=xbuf[:, par * D + 128 * ct:
                                                      par * D + 128 * (ct + 1)],
                                             in1=ps[:, 0:128])
                                v.mark(v_x3.h)
                            op("vector", fvx3)
                            rotT.release(g, v_x3, nx3)
                        NX3 = v_x3.n

                        ndo = d_o[par].bump(16)

                        def fgx3(g, til=til, par=par, nx3=NX3, R=R):
                            g.wait_ge(v_x3.h, nx3)
                            g.dma_start(out=x3d[R + 128 * til:R + 128 * (til + 1), :],
                                        in_=x3t[:, par * D:(par + 1) * D]
                                        ).then_inc(d_o[par].h, 16)
                        op("gpsimd", fgx3)

                        nst = v_st.bump()

                        def fvl3(v, par=par):
                            xb = x3t[:, par * D:(par + 1) * D]
                            st = stats[:, par * 12:par * 12 + 12]
                            v.bn_stats(out=st[:, 0:6], in_=xb[:, 0:512])
                            v.bn_stats(out=st[:, 6:12], in_=xb[:, 512:1024])
                            v.bn_aggr(out=mv[:, par * 2:par * 2 + 2], in_=st)
                            v.tensor_scalar(out=veps[:, par:par + 1],
                                            in0=mv[:, par * 2 + 1:par * 2 + 2],
                                            scalar1=1e-5, scalar2=None, op0=ALU.add)
                            v.mark(v_st.h)
                        op("vector", fvl3)

                        nrs = s_rs.bump()

                        def fsl3(sc, par=par, nst=nst, nln_prev=v_ln.n):
                            sc.wait_ge(v_st.h, nst)
                            if nln_prev >= 1:
                                sc.wait_ge(v_ln.h, nln_prev)
                            sc.activation(out=rstd[:, par:par + 1],
                                          in_=veps[:, par:par + 1], func=AF.Ln)
                            sc.activation(out=rstd[:, par:par + 1],
                                          in_=rstd[:, par:par + 1],
                                          func=AF.Exp, scale=-0.5)
                            sc.mark(s_rs.h)
                        op("scalar", fsl3)

                        tr0 = t_tr.n
                        nln = v_ln.bump()

                        def fvn3(v, par=par, nrs=nrs, tr0=tr0):
                            v.wait_ge(s_rs.h, nrs)
                            v.wait_ge(t_tr.h, tr0)
                            xb = x3t[:, par * D:(par + 1) * D]
                            v.tensor_scalar(out=lnt[:], in0=xb,
                                            scalar1=mv[:, par * 2:par * 2 + 1],
                                            scalar2=rstd[:, par:par + 1],
                                            op0=ALU.subtract, op1=ALU.mult)
                            v.mark(v_ln.h)
                        op("vector", fvn3)

                        for ct in range(NCT):
                            ps, guard, g = rotT.acquire()
                            ntr = t_tr.bump()

                            def ftz5(t, ct=ct, ps=ps, guard=guard, nln=nln):
                                if ct == 0:
                                    t.wait_ge(v_ln.h, nln)
                                wait_guard(t, guard)
                                t.transpose(ps[:, 0:128], lnt[:, 128 * ct:128 * (ct + 1)],
                                            identt[:]).then_inc(t_tr.h, 1)
                            op("tensor", ftz5)

                            nev = v_ev.bump()

                            def fvze5(v, ct=ct, ps=ps, ntr=ntr):
                                v.wait_ge(t_tr.h, ntr)
                                v.tensor_copy(out=lnzT[:, 128 * ct:128 * (ct + 1)],
                                              in_=ps[:, 0:128])
                                v.mark(v_ev.h)
                            op("vector", fvze5)
                            rotT.release(g, v_ev, nev)
                        EVZ5 = v_ev.n

                        ntz = t_z.bump()

                        def ftzm5(t, evz=EVZ5, nz_prev=v_z.n):
                            t.wait_ge(v_ev.h, evz)
                            if nz_prev >= 1:
                                t.wait_ge(v_z.h, nz_prev)
                            for ct in range(NCT):
                                mi = t.matmul(psZ[:, 0:8], lnzT[:, 128 * ct:128 * (ct + 1)],
                                              projT_s[:, 8 * ct:8 * (ct + 1)],
                                              start=(ct == 0), stop=(ct == NCT - 1))
                            mi.then_inc(t_z.h, 1)
                        op("tensor", ftzm5)

                        nvz = v_z.bump()

                        def fvzz5(v, zc=32 * bi + til, ntz=ntz,
                                  first=(bi == 0 and h == 0 and c4r == 0 and st4 == 0),
                                  xr0=XR0, dsh0=32 if exchange == "shared" else 0):
                            v.wait_ge(t_z.h, ntz)
                            if first:
                                v.wait_ge(d_z.h, xr0)
                                if dsh0:
                                    v.wait_ge(d_sh.h, dsh0)
                            v.tensor_copy(out=z_sb[:, 8 * zc:8 * (zc + 1)],
                                          in_=psZ[:, 0:8])
                            v.mark(v_z.h)
                        op("vector", fvzz5)
            wslot_guard["p4_mm"] = t_mm.n
            wslot_guard["p4_cv"] = t_cv.n

        phase("p1")
        TS_TOT = t_s.n
        XR0 = 0
        if dbg_rank == 1:
            def fdbg1(g, nvz=NVZ_R0, do0=d_o[0].n, do1=d_o[1].n):
                g.wait_ge(v_z.h, nvz)
                g.dma_start(out=zdbg, in_=z_sb[:]).then_inc(d_z.h, 16)
                g.wait_ge(d_z.h, 16)
                g.wait_ge(d_o[0].h, do0)
                g.wait_ge(d_o[1].h, do1)
            op("gpsimd", fdbg1)
        else:
            XR0 = emit_exchange(0, NVZ_R0)
            ngd0 = emit_router(XR0)
            VIT_FIN0 = emit_bisect(ngd0)
            phase("rt0")
            for seg in range(4):
                ng8 = emit_gates(16, 16 * seg)
                emit_ffn_segment("p3", seg, VIT_FIN0, ng8,
                                 extra_w_guards=([(t_s.h, TS_TOT)] if seg == 0 else ()))
            phase("p3")
            if dbg_rank == 3:
                def fdbg3(g, do0=d_o[0].n, do1=d_o[1].n):
                    g.wait_ge(d_o[0].h, do0)
                    g.wait_ge(d_o[1].h, do1)
                op("gpsimd", fdbg3)
            else:
                TY_P3 = t_y.n
                TMM_P3 = t_mm.n
                for bi in range(2):
                    emit_attn_b(bi, TY_P3, TMM_P3, VIT_FIN0)
                phase("p4")
                NVZ_R1 = v_z.n
                if dbg_rank == 4:
                    def fdbg4(g, nvz=NVZ_R1, do0=d_o[0].n, do1=d_o[1].n):
                        g.wait_ge(v_z.h, nvz)
                        g.dma_start(out=zdbg, in_=z_sb[:]).then_inc(d_z.h, XR0 and 16 or 16)
                        g.wait_ge(d_z.h, XR0 + 16)
                        g.wait_ge(d_o[0].h, do0)
                        g.wait_ge(d_o[1].h, do1)
                    op("gpsimd", fdbg4)
                else:
                    XR1 = emit_exchange(1, NVZ_R1)
                    ngd1 = emit_router(XR1)
                    VIT_FIN1 = emit_bisect(ngd1)
                    phase("rt1")
                    NG8_1 = emit_gates(16)
                    emit_ffn_segment("p6", 4, VIT_FIN1, NG8_1,
                                     extra_w_guards=[(t_mm.h, wslot_guard["p4_mm"]),
                                                     (t_cv.h, wslot_guard["p4_cv"])])

                    def ffin(g, do0=d_o[0].n, do1=d_o[1].n):
                        g.wait_ge(d_o[0].h, do0)
                        g.wait_ge(d_o[1].h, do1)
                    op("gpsimd", ffin)

        phase("end")

        def replay(ename, e):
            lst = OPS[ename]
            prev = 0
            for name, marks in PHB:
                hi = marks[ename] if name != "end" else len(lst)
                if hi > prev:
                    with nc.named_scope(name):
                        for f in lst[prev:hi]:
                            f(e)
                prev = hi
            for f in lst[prev:]:
                f(e)

        with nc.Block() as block:
            @block.sync
            def _(e):
                replay("sync", e)

            @block.gpsimd
            def _(e):
                replay("gpsimd", e)

            @block.tensor
            def _(e):
                replay("tensor", e)

            @block.vector
            def _(e):
                replay("vector", CE(e, chV))

            @block.scalar
            def _(e):
                replay("scalar", CE(e, chS))

    return nc


def prep_fused_inputs(x, q_scale, k_scale, decay_logit, out_scale, proj_w,
                      proj_b, expert_bias, fc1_w, fc1_b, fc2_w, fc2_b,
                      sh1_w, sh1_b, sh2_w, sh2_b, attnb, with_nonce=True):
    f32 = np.float32
    qs_cm = np.ascontiguousarray(np.asarray(q_scale, f32).reshape(NCT, 128).T)
    ks_cm = np.ascontiguousarray(np.asarray(k_scale, f32).reshape(NCT, 128).T)
    pT = np.asarray(proj_w, f32).T
    projT_h = np.ascontiguousarray(pT.reshape(NCT, 128, 8).transpose(1, 0, 2)
                                   .reshape(128, NCT * 8))
    proj_b = np.asarray(proj_b, f32).reshape(8)
    expert_bias = np.asarray(expert_bias, f32).reshape(8)
    pb8 = np.broadcast_to(proj_b[None, :], (128, 8)).copy()
    eb8 = np.broadcast_to(expert_bias[None, :], (128, 8)).copy()
    u1 = np.zeros((NU, D, D), f32)
    u2 = np.zeros((NU, D, D), f32)
    ub = np.zeros((128, NU * 8), f32)
    for ui, unit in enumerate(UNITS):
        if unit[0] == "sh":
            q = unit[1]
            u1[ui] = sh1_w[1024 * q:1024 * (q + 1), :].T
            u2[ui] = sh2_w[:, 1024 * q:1024 * (q + 1)].T
            bb = sh1_b[1024 * q:1024 * (q + 1)]
        else:
            e, h = unit[1], unit[2]
            u1[ui] = fc1_w[e, 1024 * h:1024 * (h + 1), :].T
            u2[ui] = fc2_w[e][:, 1024 * h:1024 * (h + 1)].T
            bb = fc1_b[e, 1024 * h:1024 * (h + 1)]
        ub[:, ui * 8:(ui + 1) * 8] = bb.reshape(8, 128).T

    def cm(v, nct):
        return np.ascontiguousarray(np.asarray(v, f32).reshape(nct, 128).T)

    a = attnb
    kern = np.asarray(a["kernel"], f32)
    base = {
        "qs_cm": qs_cm, "ks_cm": ks_cm,
        "scal": np.array([[float(decay_logit), float(out_scale), 0.0, 0.0]], f32),
        "projT": projT_h, "pb8": pb8, "eb8": eb8,
        "u1wb": u1.astype(bfd), "u2wb": u2.astype(bfd), "ubias": ub,
        "fc2bT": np.asarray(fc2_b, f32), "sh2b_row": np.asarray(sh2_b, f32).reshape(1, D),
        "in_wb": np.asarray(a["in_w"], f32).astype(bfd), "in_b_cm": cm(a["in_b"], 8),
        "mod_w1b": np.asarray(a["mod_w1"], f32).astype(bfd), "mod_b1_cm": cm(a["mod_b1"], 32),
        "mod_w2b": np.asarray(a["mod_w2"], f32).astype(bfd), "mod_b2_cm": cm(a["mod_b2"], 8),
        "kern_cm": np.ascontiguousarray(
            kern.reshape(8, 128, 64).transpose(1, 0, 2).reshape(128, 512)),
        "out_wb": np.asarray(a["out_w"], f32).astype(bfd), "out_b_cm": cm(a["out_b"], 8),
    }
    if with_nonce:
        rng = np.random.default_rng()
        base["nonce"] = np.array([[rng.integers(1, 2**31), 0, 0, 0]], np.uint32)
    maps = []
    for c in range(8):
        b0 = 0 if c % 2 == 0 else 2
        m = dict(base)
        m["xin"] = np.ascontiguousarray(
            np.asarray(x, np.float32)[b0:b0 + 2].reshape(8192, D))
        maps.append(m)
    return maps


_NC_CACHE = {}


def _get_nc():
    if "f" not in _NC_CACHE:
        _NC_CACHE["f"] = build_fused(cora=None, exchange="shared", rc=False, dbg=None)
    return _NC_CACHE["f"]


def kernel(**inputs):
    from concourse.bass_utils import run_bass_kernel_spmd
    f32 = np.float32
    g = lambda k: np.asarray(inputs[k], f32)
    x = g("L0_attn_a_x")
    dlog = float(np.asarray(inputs["L0_attn_a_decay_logit"]))
    osc = float(np.asarray(inputs["L0_attn_a_out_scale"]))
    attnb = {
        "in_w": g("L0_attn_b_in_w"), "in_b": g("L0_attn_b_in_b"),
        "mod_w1": g("L0_attn_b_mod_w1"), "mod_b1": g("L0_attn_b_mod_b1"),
        "mod_w2": g("L0_attn_b_mod_w2"), "mod_b2": g("L0_attn_b_mod_b2"),
        "kernel": g("L0_attn_b_kernel"), "out_w": g("L0_attn_b_out_w"),
        "out_b": g("L0_attn_b_out_b"),
    }
    nc = _get_nc()
    maps = prep_fused_inputs(
        x, g("L0_attn_a_q_scale"), g("L0_attn_a_k_scale"), dlog, osc,
        g("L0_ffn_proj_w"), g("L0_ffn_proj_b"), g("L0_ffn_expert_bias"),
        g("L0_ffn_fc1_w"), g("L0_ffn_fc1_b"), g("L0_ffn_fc2_w"), g("L0_ffn_fc2_b"),
        g("L0_ffn_sh1_w"), g("L0_ffn_sh1_b"), g("L0_ffn_sh2_w"), g("L0_ffn_sh2_b"),
        attnb, with_nonce=True)
    res = run_bass_kernel_spmd(nc, maps, core_ids=list(range(8)))
    y = np.zeros((4, 4096, 1024), f32)
    for c in range(8):
        bsel = (2 if c % 2 else 0) + (c // 2) // 2
        hsel = (c // 2) % 2
        y[bsel, 2048 * hsel:2048 * (hsel + 1)] = res.results[c]["y_out"]
    return y
